# version 11
# baseline (speedup 1.0000x reference)
import numpy as np
import ml_dtypes

import concourse.bass as bass
import concourse.mybir as mybir
from concourse.bass_utils import run_bass_kernel_spmd

F32 = mybir.dt.float32
BF16 = mybir.dt.bfloat16
I8 = mybir.dt.int8
U8 = mybir.dt.uint8
AF = mybir.ActivationFunctionType
ALU = mybir.AluOpType
AX = mybir.AxisListType.X
BF = ml_dtypes.bfloat16

T = 4096
D = 1024
NT = T // 128
ND = D // 128
NQ = T // 512
NKV = T // 128
EPS = 1e-5
MAGIC = float(np.float32(3 * 2 ** 22))
KB = 1024

DEBUG = False
CONV_SERIAL = False
CKV_NBANK = 4
PHASES = ("attn", "ffn1", "attnb", "ffn2")


class Sem:
    def __init__(self, h):
        self.h = h
        self.n = 0


class Prog:
    ENGS = ("tensor", "vector", "scalar", "gpsimd", "sync")

    def __init__(self, nc):
        self.nc = nc
        self.q = {e: [] for e in self.ENGS}
        self._nsem = 0

    def sem(self):
        self._nsem += 1
        return Sem(self.nc._stack.enter_context(
            self.nc.semaphore(name=f"sem{self._nsem}")))

    def on(self, eng, fn, waits=(), inc=None, inc_k=1):
        ws = {}
        for w in waits:
            if isinstance(w, Sem):
                s, v = w, w.n
            else:
                s, v = w
            assert v <= s.n, f"wait on future value {v} > {s.n}"
            if v > 0:
                ws[s.h] = max(ws.get(s.h, 0), v)
        if inc is not None:
            inc.n += inc_k
        self.q[eng].append((list(ws.items()), fn,
                            None if inc is None else (inc.h, inc_k)))

    def emit(self):
        with self.nc.Block() as blk:
            for eng in self.ENGS:
                ops = self.q[eng]
                if not ops:
                    continue

                def body(e, ops=ops):
                    for ws, fn, inc in ops:
                        for (h, v) in ws:
                            e.wait_ge(h, v)
                        inst = fn(e) if fn is not None else None
                        if inc is not None:
                            if inst is None:
                                inst = e.nop()
                            inst.then_inc(inc[0], inc[1])
                getattr(blk, eng)(body)


class Arena:
    def __init__(self, nc, total_bytes):
        self.nc = nc
        self.total = total_bytes
        self.back = nc._stack.enter_context(
            nc.sbuf_tensor("arena", [128, total_bytes], U8))
        self.base = nc.lookup_mloc(self.back).addr

    def at(self, name, addr, shape, dtype):
        per_part = int(np.prod(shape[1:])) * mybir.dt.size(dtype)
        assert addr % 32 == 0, (name, addr)
        assert addr + per_part <= self.total, (name, addr, per_part)
        return self.nc.alloc_sbuf_tensor_at(
            name, shape, dtype, offset=self.base + addr)


def _ap(t, offset, dims):
    return bass.AP(t, offset, [list(d) for d in dims])


def host_consts():
    DH = 64
    inv_freq = 1.0 / (10000.0 ** (np.arange(0, DH, 2, dtype=np.float64) / DH))
    freqs = np.arange(T, dtype=np.float64)[:, None] * inv_freq[None, :]
    emb = np.concatenate([freqs, freqs], axis=-1)
    costab = np.tile(np.cos(emb).astype(np.float32).T, (2, 1))
    sintab = np.tile(np.sin(emb).astype(np.float32).T, (2, 1))
    PT = np.zeros((128, 128), np.float32)
    for h in range(2):
        for dp in range(32):
            PT[h * 64 + dp, h * 64 + dp + 32] = -1.0
        for dp in range(32, 64):
            PT[h * 64 + dp, h * 64 + dp - 32] = 1.0
    pmat = np.ascontiguousarray(PT.T).astype(BF)
    ident = np.eye(128, dtype=BF)
    identf = np.eye(128, dtype=np.float32)
    return costab, sintab, pmat, ident, identf


def build():
    from contextlib import ExitStack
    nc = bass.Bass("TRN2", target_bir_lowering=False, debug=False)
    nc._stack = ExitStack()
    dt_ = nc.dram_tensor

    x_in = dt_("x", [T, D], F32, kind="ExternalInput")
    w_dk = dt_("w_dk", [D, 128], F32, kind="ExternalInput")
    b_dk = dt_("b_dk", [128], F32, kind="ExternalInput")
    w_dv = dt_("w_dv", [D, 128], F32, kind="ExternalInput")
    b_dv = dt_("b_dv", [128], F32, kind="ExternalInput")
    w_uk = dt_("w_uk", [128, D], F32, kind="ExternalInput")
    b_uk = dt_("b_uk", [D], F32, kind="ExternalInput")
    w_uv = dt_("w_uv", [128, D], F32, kind="ExternalInput")
    b_uv = dt_("b_uv", [D], F32, kind="ExternalInput")
    w_o = dt_("w_o", [D, D], F32, kind="ExternalInput")
    b_o = dt_("b_o", [D], F32, kind="ExternalInput")
    W_in = dt_("W_in", [D, 3 * D], F32, kind="ExternalInput")
    b_in = dt_("b_in", [3 * D], F32, kind="ExternalInput")
    k0 = dt_("k0", [D, 128], F32, kind="ExternalInput")
    k1 = dt_("k1", [D, 128], F32, kind="ExternalInput")
    gamma = dt_("gamma", [D], F32, kind="ExternalInput")
    beta = dt_("beta", [D], F32, kind="ExternalInput")
    W_out = dt_("W_out", [D, D], F32, kind="ExternalInput")
    b_out = dt_("b_out", [D], F32, kind="ExternalInput")
    ffn_w8 = dt_("ffn_w8", [D, D], I8, kind="ExternalInput")
    ffn_ws = dt_("ffn_ws", [D], F32, kind="ExternalInput")
    ffn_b = dt_("ffn_b", [D], F32, kind="ExternalInput")
    costab_in = dt_("costab", [128, T], F32, kind="ExternalInput")
    sintab_in = dt_("sintab", [128, T], F32, kind="ExternalInput")
    pmat_in = dt_("pmat", [128, 128], BF16, kind="ExternalInput")
    ident_in = dt_("ident", [128, 128], BF16, kind="ExternalInput")
    identf_in = dt_("identf", [128, 128], F32, kind="ExternalInput")
    out_t = dt_("out", [T, D], F32, kind="ExternalOutput")

    r1 = dt_("r1", [T, D], F32)
    r2 = dt_("r2", [T, D], F32)
    r3 = dt_("r3", [T, D], F32)
    z3 = dt_("z3", [T, D], F32)
    xnbc = dt_("xnbc", [NT, 128, ND * 128], BF16)
    qrope = dt_("qrope", [ND, 128, T], BF16)
    kfwd = dt_("kfwd", [2, D, 384], BF16)
    toep = dt_("toep", [2, D, 128, 256], BF16)
    wo_bf_d = dt_("wo_bf", [ND, 128, D], BF16)

    dbg = {}
    if DEBUG == "attnb":
        dbg["xnbc"] = dt_("dbg_xnbc", [NT, 128, ND * 128], BF16, kind="ExternalOutput")
        dbg["z3"] = dt_("dbg_z3", [T, D], F32, kind="ExternalOutput")
        dbg["vt"] = dt_("dbg_vt", [128, 1024 * 33], BF16, kind="ExternalOutput")
        dbg["x1t"] = dt_("dbg_x1t", [128, 1024 * 33], BF16, kind="ExternalOutput")
        dbg["y1"] = dt_("dbg_y1", [128, 1024 * 33], BF16, kind="ExternalOutput")
        dbg["y2"] = dt_("dbg_y2", [128, 1024 * 33], BF16, kind="ExternalOutput")
        dbg["x2t"] = dt_("dbg_x2t", [128, 1024 * 33], BF16, kind="ExternalOutput")
    if DEBUG in ("sc", "att1") or DEBUG is True:
        dbg["sc"] = dt_("dbg_sc", [1, 8], F32, kind="ExternalOutput")
        dbg["r1"] = dt_("dbg_r1", [T, D], F32, kind="ExternalOutput")
        dbg["r2"] = dt_("dbg_r2", [T, D], F32, kind="ExternalOutput")
        dbg["r3"] = dt_("dbg_r3", [T, D], F32, kind="ExternalOutput")
    if DEBUG is True or DEBUG == "att1":
        dbg["xn1c"] = dt_("dbg_xn1c", [128, ND * T], BF16, kind="ExternalOutput")
        dbg["ck"] = dt_("dbg_ck", [128, T], BF16, kind="ExternalOutput")
        dbg["krope"] = dt_("dbg_krope", [128, ND * T], BF16, kind="ExternalOutput")
        dbg["vext"] = dt_("dbg_vext", [128, NKV * 1040], BF16, kind="ExternalOutput")
        dbg["buv"] = dt_("dbg_buv", [1, D], F32, kind="ExternalOutput")
    if DEBUG is True:
        dbg["qr"] = dt_("dbg_qr", [128, ND * T], BF16, kind="ExternalOutput")

    P = Prog(nc)
    A = Arena(nc, 192 * KB)

    ident_sb = A.at("identsb", 0, [128, 128], BF16)
    pmat_sb = A.at("pmatsb", 256, [128, 128], BF16)
    onesf = A.at("onesf", 512, [1, 128], F32)
    identf_sb = A.at("identfsb", 1024, [128, 128], F32)
    bdk_sb = A.at("bdksb", 1536, [128, 1], F32)
    bdv_sb = A.at("bdvsb", 1568, [128, 1], F32)
    buk_sb = A.at("buksb", 1600, [128, 8], F32)
    wsc_sb = A.at("wscsb", 1632, [128, 8], F32)
    bff_sb = A.at("bffsb", 1664, [128, 8], F32)
    ssw_sb = A.at("sswsb", 1696, [128, 8], F32)
    gam_sb = A.at("gamsb", 1728, [128, 8], F32)
    bout_sb = A.at("boutsb", 1760, [128, 8], F32)
    beta_sb = A.at("betasb", 1792, [128, 8], F32)
    mgc_sb = A.at("mgcsb", 1824, [128, 1], F32)
    invs_sb = A.at("invssb", 1856, [128, 1], F32)
    sml = A.at("sml", 1888, [128, 24], F32)
    wdk_sb = A.at("wdksb", 2048, [128, ND * 128], BF16)
    wdv_sb = A.at("wdvsb", 4096, [128, ND * 128], BF16)
    wuk_sb = A.at("wuksb", 6144, [128, D], BF16)
    wuv_sb = A.at("wuvsb", 8192, [128, D], BF16)
    borow = A.at("borow", 10240, [1, D], F32)

    maxs_sb = A.at("maxssb", 14336, [128, 32], F32)
    wstg = A.at("wstg", 14464, [128, 256], F32)
    wcst = A.at("wcst", 15488, [128, 256], BF16)

    s4 = nc._stack.enter_context(nc.psum_tensor("ps_s4", [128, 2048], F32))
    o2 = nc._stack.enter_context(nc.psum_tensor("ps_o2", [128, 1024], F32))
    aux = nc._stack.enter_context(nc.psum_tensor("ps_aux", [128, 512], F32))
    pb = nc._stack.enter_context(nc.psum_tensor("ps_pb", [128, 1024], BF16))

    TB = 16 * KB
    tabc = A.at("tabc", TB, [128, 2 * 512], F32)
    tabs_ = A.at("tabs", TB + 4096, [128, 2 * 512], F32)
    ck_sb = A.at("cksb", 24576, [128, T], BF16)
    cv_sb = A.at("cvsb", 32768, [128, T], BF16)
    xn1c = A.at("xn1c", 40960, [128, ND * T], BF16)
    krope = A.at("krope", 40960, [128, ND * T], BF16)
    vext = A.at("vext", 106496, [128, NKV * 1040], BF16)
    tmp1 = A.at("tmp1", 173056, [128, 512], F32)
    tmp2 = A.at("tmp2", 175104, [128, 512], F32)
    ktmp = A.at("ktmp", 177152, [128, 512], BF16)
    lnstg = A.at("lnstg", 178176, [128, 1024], BF16)
    xtile = A.at("xtile", 180224, [128, 2 * 1024], F32)
    x2stg = A.at("x2stg", 188416, [128, 2 * 1024], F32)
    bdk_row = A.at("bdk_row", 188416, [1, 128], F32)
    bdv_row = A.at("bdv_row", 188928, [1, 128], F32)
    buk_row = A.at("buk_row", 189440, [1, D], F32)
    ones512 = A.at("ones512", 193536, [1, 512], F32)
    sqs = A.at("sqs", 24576, [128, 2 * 1024], F32)
    kstg = A.at("kstg", 33792, [128, 128], F32)
    kfb = A.at("kfb", 34816, [128, 384], BF16)
    buvrow = A.at("buvrow", 173056, [1, D], F32)
    qpair = A.at("qpair", TB, [128, 2 * 512], BF16)
    p_sb = A.at("p_sb", TB + 2048, [128, 3 * 1024], BF16)
    oc_sb = A.at("oc_sb", TB + 8192, [128, ND * 512], BF16)
    rbc_sb = A.at("rbc_sb", TB + 16384, [128, 512], F32)
    wob = A.at("wob", TB + 18432, [128, 2 * 1024], BF16)
    rr_sb = A.at("rr_sb", 175104, [1, 1024], F32)

    def pfence(out=None, waits=(), inc=None):
        P.on("tensor", lambda e, out=out: e.matmul(
            out if out is not None else aux[0:1, 0:1],
            ident_sb[:, 0:1], ident_sb[:, 0:1],
            start=True, stop=True), waits=waits, inc=(inc if inc else s_te))

    def tfence(waits=()):
        pfence(waits=waits)

    s_dma = P.sem()
    lnx = [P.sem(), P.sem()]
    tabsm = [P.sem(), P.sem()]
    qpsm = [P.sem(), P.sem()]
    wobsm = [P.sem(), P.sem()]
    fxnsm = [P.sem(), P.sem()]
    fx2sm = [P.sem(), P.sem()]
    z3sm = [P.sem(), P.sem()]
    bxbsm = [P.sem(), P.sem()]
    tpsm = [P.sem() for _ in range(8)]
    wstsm = P.sem()
    kstsm = P.sem()
    s_qwh = [P.sem(), P.sem()]
    s_kf = P.sem()
    s_tp = P.sem()
    s_wob = P.sem()
    s_r1h = [P.sem(), P.sem()]
    s_dsth = [P.sem(), P.sem()]
    s_z3h = [P.sem(), P.sem()]
    s_xbst = P.sem()
    s_xch = [P.sem(), P.sem()]
    s_r3h = [P.sem(), P.sem()]
    s_ccio = P.sem()
    s_scd = P.sem()
    s_sto = P.sem()
    s_ve = P.sem()
    s_ac = P.sem()
    s_te = P.sem()
    s_qk = P.sem()
    s_exp = P.sem()
    s_pv = P.sem()
    s_ev = P.sem()
    s_wo = P.sem()

    def ld(e, dst, src, **kw):
        return e.dma_start(out=dst, in_=src, **kw)

    P.on("sync", lambda e: ld(e, ident_sb[:], ident_in.ap()), inc=s_dma, inc_k=16)
    P.on("sync", lambda e: ld(e, pmat_sb[:], pmat_in.ap()), inc=s_dma, inc_k=16)
    P.on("sync", lambda e: ld(e, identf_sb[:], identf_in.ap()), inc=s_dma, inc_k=16)
    for t, src in ((buk_sb, b_uk), (wsc_sb, ffn_ws), (bff_sb, ffn_b),
                   (gam_sb, gamma), (bout_sb, b_out), (beta_sb, beta)):
        P.on("sync", lambda e, t=t, src=src: ld(
            e, t[:], _ap(src, 0, [[1, 128], [128, 8]]),
            allow_slow_non_contiguous=True), inc=s_dma, inc_k=16)
    P.on("sync", lambda e: ld(e, bdk_sb[:], _ap(b_dk, 0, [[1, 128], [1, 1]]),
                              allow_slow_non_contiguous=True), inc=s_dma, inc_k=16)
    P.on("sync", lambda e: ld(e, bdv_sb[:], _ap(b_dv, 0, [[1, 128], [1, 1]]),
                              allow_slow_non_contiguous=True), inc=s_dma, inc_k=16)
    P.on("sync", lambda e: ld(e, borow[:], _ap(b_o, 0, [[0, 1], [1, 1024]])),
         inc=s_dma, inc_k=16)
    P.on("vector", lambda e: e.memset(onesf[:], 1.0), inc=s_ve)
    P.on("vector", lambda e: e.memset(ones512[:], 1.0), inc=s_ve)
    P.on("vector", lambda e: e.memset(mgc_sb[:], MAGIC), inc=s_ve)
    P.on("sync", lambda e: ld(e, bdk_row[:], _ap(b_dk, 0, [[128, 1], [1, 128]])),
         inc=s_dma, inc_k=16)
    P.on("sync", lambda e: ld(e, bdv_row[:], _ap(b_dv, 0, [[128, 1], [1, 128]])),
         inc=s_dma, inc_k=16)
    P.on("sync", lambda e: ld(e, buk_row[:], _ap(b_uk, 0, [[1024, 1], [1, 1024]])),
         inc=s_dma, inc_k=16)
    prep_t = s_dma.n

    wdkf = A.at("wdkf", 106496, [128, 1024], F32)
    wdvf = A.at("wdvf", 110592, [128, 1024], F32)
    wukf = A.at("wukf", 114688, [128, 1024], F32)
    wuvf = A.at("wuvf", 118784, [128, 1024], F32)
    wof = A.at("wof", 122880, [128, 8 * 1024], F32)
    wob16 = A.at("wob16", 155648, [128, 8 * 1024], BF16)
    kfb2 = A.at("kfb2", 122880, [128, 2 * 384], BF16)
    kstg2 = A.at("kstg2", 16384, [128, 2 * 1024], F32)
    bigld = P.sem()
    P.on("sync", lambda e: ld(
        e, wdkf[:], _ap(w_dk, 0, [[128, 128], [128 * 128, 8], [1, 128]])),
        inc=bigld, inc_k=16)
    P.on("sync", lambda e: ld(
        e, wdvf[:], _ap(w_dv, 0, [[128, 128], [128 * 128, 8], [1, 128]])),
        inc=bigld, inc_k=16)
    P.on("sync", lambda e: ld(
        e, wukf[:], _ap(w_uk, 0, [[1024, 128], [1, 1024]])), inc=bigld, inc_k=16)
    P.on("sync", lambda e: ld(
        e, wuvf[:], _ap(w_uv, 0, [[1024, 128], [1, 1024]])), inc=bigld, inc_k=16)
    P.on("sync", lambda e: ld(
        e, wof[:], _ap(w_o, 0, [[1024, 128], [128 * 1024, 8], [1, 1024]])),
        inc=bigld, inc_k=16)
    P.on("sync", lambda e: ld(
        e, kstg2[:, 0:1024], _ap(k0, 0, [[128, 128], [128 * 128, 8], [1, 128]])),
        inc=bigld, inc_k=16)
    P.on("sync", lambda e: ld(
        e, kstg2[:, 1024:2048], _ap(k1, 0, [[128, 128], [128 * 128, 8], [1, 128]])),
        inc=bigld, inc_k=16)
    bld_all = bigld.n
    for dst_sb, stg in ((wdk_sb, wdkf), (wdv_sb, wdvf),
                        (wuk_sb, wukf), (wuv_sb, wuvf)):
        P.on("vector", lambda e, dst_sb=dst_sb, stg=stg: e.tensor_copy(
            dst_sb[:], stg[:]), waits=((bigld, bld_all),), inc=s_ve)
    wcast_gp = 0

    P.on("vector", lambda e: e.tensor_copy(wob16[:], wof[:]),
         waits=((bigld, bld_all),), inc=s_ve)
    wo_cv = s_ve.n
    P.on("gpsimd", lambda e: ld(
        e, _ap(wo_bf_d, 0, [[1024, 128], [128 * 1024, 8], [1, 1024]]), wob16[:]),
        waits=((s_ve, wo_cv),), inc=s_wob, inc_k=16)

    kf_st = [0, 0]
    for it in range(16):
        oi, c8 = it // 8, it % 8
        sl = it % 2
        P.on("vector", lambda e, sl=sl: e.memset(
            kfb2[:, sl * 384:(sl + 1) * 384], 0.0),
            waits=((s_kf, kf_st[sl]),) if kf_st[sl] else (), inc=s_ve)
        P.on("vector", lambda e, oi=oi, c8=c8, sl=sl: e.tensor_copy(
            kfb2[:, sl * 384 + 127: sl * 384 + 255],
            kstg2[:, oi * 1024 + c8 * 128: oi * 1024 + (c8 + 1) * 128]),
            waits=((bigld, bld_all), (s_ve, s_ve.n)), inc=s_ve)
        kf_c = s_ve.n
        P.on("gpsimd", lambda e, oi=oi, c8=c8, sl=sl: ld(
            e, _ap(kfwd, (oi * D + c8 * 128) * 384, [[384, 128], [1, 384]]),
            kfb2[:, sl * 384:(sl + 1) * 384]), waits=((s_ve, kf_c),),
            inc=s_kf, inc_k=16)
        kf_st[sl] = s_kf.n
    kf_done = s_kf.n

    toep_dmas = [(oi, p_) for oi in range(2) for p_ in range(128)]

    def emit_toep(n):
        for _ in range(n):
            if not toep_dmas:
                return
            oi, p_ = toep_dmas.pop(0)
            P.on("gpsimd", lambda e, oi=oi, p_=p_: ld(
                e, _ap(toep, (oi * D) * 128 * 256 + p_ * 256,
                       [[128 * 256, 1024], [1, 256]]),
                _ap(kfwd, (oi * D) * 384 + 127 - p_, [[384, 1024], [1, 256]])),
                waits=((s_kf, kf_done),), inc=s_tp, inc_k=16)

    ln_cons = [0, 0]
    sq_red = [0, 0]
    ln_tp = [0]
    for i in range(NT):
        par = i % 2
        sm = i % 2
        w_war = ((s_ve, ln_cons[par]),) if i >= 2 else ()
        P.on("sync", lambda e, i=i, par=par: ld(
            e, xtile[:, par * 1024:(par + 1) * 1024],
            x_in.ap()[i * 128:(i + 1) * 128, :]),
            waits=w_war, inc=lnx[par], inc_k=16)
        xld = lnx[par].n
        xt = xtile[:, par * 1024:(par + 1) * 1024]
        P.on("vector", lambda e, xt=xt, sm=sm: e.reduce_sum(
            sml[:, sm:sm + 1], xt, axis=AX),
            waits=((lnx[par], xld), (s_dma, prep_t)), inc=s_ve)
        sq_w = [(lnx[par], xld), (s_ve, sq_red[par])]
        P.on("scalar", lambda e, xt=xt, par=par: e.activation(
            sqs[:, par * 1024:(par + 1) * 1024], xt, AF.Square),
            waits=tuple(sq_w) + ((s_dma, prep_t),), inc=s_ac)
        P.on("vector", lambda e, sm=sm, par=par: e.reduce_sum(
            sml[:, 2 + sm:3 + sm], sqs[:, par * 1024:(par + 1) * 1024], axis=AX),
            waits=(s_ac,), inc=s_ve)
        sq_red[par] = s_ve.n
        P.on("vector", lambda e, sm=sm: e.tensor_scalar(
            sml[:, 4 + sm:5 + sm], sml[:, sm:sm + 1], -1.0 / D, None, ALU.mult),
            waits=(s_ve,), inc=s_ve)
        P.on("vector", lambda e, sm=sm: e.tensor_mul(
            sml[:, 6 + sm:7 + sm], sml[:, sm:sm + 1], sml[:, sm:sm + 1]),
            waits=(s_ve,), inc=s_ve)
        P.on("vector", lambda e, sm=sm: e.tensor_scalar(
            sml[:, 8 + sm:9 + sm], sml[:, 2 + sm:3 + sm], 1.0 / D, EPS,
            ALU.mult, ALU.add), waits=(s_ve,), inc=s_ve)
        P.on("vector", lambda e, sm=sm: e.scalar_tensor_tensor(
            sml[:, 10 + sm:11 + sm], sml[:, 6 + sm:7 + sm], -1.0 / (D * D),
            sml[:, 8 + sm:9 + sm], ALU.mult, ALU.add), waits=(s_ve,), inc=s_ve)
        P.on("scalar", lambda e, sm=sm: e.activation(
            sml[:, 12 + sm:13 + sm], sml[:, 10 + sm:11 + sm], AF.Ln),
            waits=(s_ve,), inc=s_ac)
        P.on("scalar", lambda e, sm=sm: e.activation(
            sml[:, 14 + sm:15 + sm], sml[:, 12 + sm:13 + sm], AF.Exp, scale=-0.5),
            waits=(s_ac,), inc=s_ac)
        P.on("vector", lambda e, xt=xt, sm=sm: e.tensor_scalar(
            lnstg[:], xt, sml[:, 4 + sm:5 + sm], sml[:, 14 + sm:15 + sm],
            ALU.add, ALU.mult),
            waits=((s_ac, s_ac.n), (s_te, ln_tp[0])), inc=s_ve)
        nrm = s_ve.n
        ln_cons[par] = s_ve.n
        P.on("tensor", lambda e: [e.transpose(
            pb[:, k * 128:(k + 1) * 128], lnstg[:, k * 128:(k + 1) * 128],
            ident_sb[:]) for k in range(8)][-1],
            waits=((s_ve, nrm), (s_ve, s_ve.n), (s_dma, prep_t)), inc=None)
        tfence()
        ln_tp[0] = s_te.n
        P.on("vector", lambda e, i=i: e.tensor_copy(
            _ap(xn1c, i * 128, [[ND * T, 128], [T, 8], [1, 128]]), pb[:]),
            waits=(s_te,), inc=s_ve)
    ln1_v = s_ve.n
    if DEBUG is True or DEBUG == "att1":
        P.on("gpsimd", lambda e: ld(e, dbg["xn1c"].ap(), xn1c[:]),
             waits=((s_ve, ln1_v),), inc=s_sto, inc_k=16)

    for dk in range(ND):
        for qt in range(NQ):
            it = dk * NQ + qt
            par = it % 2
            P.on("sync", lambda e, qt=qt, par=par: ld(
                e, tabc[:, par * 512:(par + 1) * 512],
                costab_in.ap()[:, qt * 512:(qt + 1) * 512]),
                waits=((s_ve, s_ve.n),), inc=tabsm[par], inc_k=16)
            P.on("sync", lambda e, qt=qt, par=par: ld(
                e, tabs_[:, par * 512:(par + 1) * 512],
                sintab_in.ap()[:, qt * 512:(qt + 1) * 512]), inc=tabsm[par], inc_k=16)
            dma_t = (tabsm[par], tabsm[par].n)
            xsl = xn1c[:, dk * T + qt * 512: dk * T + qt * 512 + 512]
            P.on("tensor", lambda e, xsl=xsl, par=par: e.matmul(
                o2[:, par * 512:(par + 1) * 512], pmat_sb[:], xsl,
                start=True, stop=True),
                waits=((s_ve, s_ve.n),), inc=s_te)
            pfence()
            pm_t = s_te.n
            P.on("vector", lambda e, xsl=xsl, par=par: e.tensor_mul(
                tmp1[:], xsl, tabc[:, par * 512:(par + 1) * 512]),
                waits=(dma_t, (s_ve, s_ve.n)), inc=s_ve)
            P.on("vector", lambda e, par=par: e.tensor_mul(
                tmp2[:], o2[:, par * 512:(par + 1) * 512],
                tabs_[:, par * 512:(par + 1) * 512]),
                waits=((s_te, pm_t), (s_ve, s_ve.n)), inc=s_ve)
            P.on("vector", lambda e, par=par: e.tensor_add(
                _ap(lnstg, par * 512, [[1024, 128], [1, 512]]), tmp1[:], tmp2[:]),
                waits=((s_ve, s_ve.n), (s_qwh[par], s_qwh[par].n)), inc=s_ve)
            qd = s_ve.n
            P.on("gpsimd", lambda e, dk=dk, qt=qt, par=par: ld(
                e, _ap(qrope, (dk * 128) * T + qt * 512, [[T, 128], [1, 512]]),
                _ap(lnstg, par * 512, [[1024, 128], [1, 512]])),
                waits=((s_ve, qd),), inc=s_qwh[par], inc_k=16)
    if DEBUG is True:
        for dk in range(ND):
            P.on("sync", lambda e, dk=dk: ld(
                e, _ap(lnstg, 0, [[1024, 128], [1, 512]]),
                _ap(qrope, dk * 128 * T, [[T, 128], [1, 512]])),
                waits=((s_sto, s_sto.n),), inc=s_dma, inc_k=16)
            P.on("gpsimd", lambda e, dk=dk: ld(
                e, _ap(dbg["qr"], dk * T, [[ND * T, 128], [1, 512]]),
                _ap(lnstg, 0, [[1024, 128], [1, 512]])),
                waits=(s_dma,), inc=s_sto, inc_k=16)

    bank_ev = [0, 0, 0, 0]
    for wsb, _bsb, dst in ((wdk_sb, bdk_sb, ck_sb), (wdv_sb, bdv_sb, cv_sb)):
        brow = bdk_row if dst is ck_sb else bdv_row
        for ts in range(NQ):
            bank = ts % 4
            for dk in range(ND):
                w = [(s_ve, ln1_v)]
                if dk == 0 and bank_ev[bank] > 0:
                    w.append((s_ac, bank_ev[bank]))
                P.on("tensor", lambda e, wsb=wsb, dk=dk, ts=ts, bank=bank: e.matmul(
                    s4[:, bank * 512:(bank + 1) * 512],
                    wsb[:, dk * 128:(dk + 1) * 128],
                    xn1c[:, dk * T + ts * 512: dk * T + ts * 512 + 512],
                    start=(dk == 0), stop=False),
                    waits=tuple(w), inc=None)
            P.on("tensor", lambda e, brow=brow, bank=bank: e.matmul(
                s4[:, bank * 512:(bank + 1) * 512], brow[0:1, :],
                ones512[0:1, :], start=False, stop=True), inc=s_te)
            pfence()
            P.on("scalar", lambda e, dst=dst, ts=ts, bank=bank: e.activation(
                dst[:, ts * 512:(ts + 1) * 512], s4[:, bank * 512:(bank + 1) * 512],
                AF.Identity),
                waits=(s_te,), inc=s_ac)
            bank_ev[bank] = s_ac.n
    ckv_ac = s_ac.n
    if DEBUG is True or DEBUG == "att1":
        P.on("gpsimd", lambda e: ld(e, dbg["ck"].ap(), ck_sb[:]),
             waits=((s_ac, ckv_ac),), inc=s_sto, inc_k=16)

    bank_cons = [0, 0, 0, 0]
    for dk in range(ND):
        for ts in range(NQ):
            it = dk * NQ + ts
            par = it % 2
            bank = it % 4
            P.on("sync", lambda e, ts=ts, par=par: ld(
                e, tabc[:, par * 512:(par + 1) * 512],
                costab_in.ap()[:, ts * 512:(ts + 1) * 512]),
                waits=((s_ve, s_ve.n),), inc=tabsm[par], inc_k=16)
            P.on("sync", lambda e, ts=ts, par=par: ld(
                e, tabs_[:, par * 512:(par + 1) * 512],
                sintab_in.ap()[:, ts * 512:(ts + 1) * 512]), inc=tabsm[par], inc_k=16)
            dma_t = (tabsm[par], tabsm[par].n)
            w = [(s_ac, ckv_ac)]
            if bank_cons[bank] > 0:
                w.append((s_ac, bank_cons[bank]))
            P.on("tensor", lambda e, dk=dk, ts=ts, bank=bank: e.matmul(
                s4[:, bank * 512:(bank + 1) * 512],
                wuk_sb[:, dk * 128:(dk + 1) * 128],
                ck_sb[:, ts * 512:(ts + 1) * 512], start=True, stop=False),
                waits=tuple(w), inc=None)
            P.on("tensor", lambda e, dk=dk, bank=bank: e.matmul(
                s4[:, bank * 512:(bank + 1) * 512],
                buk_row[0:1, dk * 128:(dk + 1) * 128],
                ones512[0:1, :], start=False, stop=True), inc=s_te)
            pfence()
            kb_t = s_te.n
            P.on("scalar", lambda e, dk=dk, bank=bank: e.activation(
                ktmp[:], s4[:, bank * 512:(bank + 1) * 512], AF.Identity),
                waits=((s_te, kb_t), (s_ve, s_ve.n)), inc=s_ac)
            kev = s_ac.n
            bank_cons[bank] = kev
            P.on("tensor", lambda e, par=par: e.matmul(
                o2[:, par * 512:(par + 1) * 512], pmat_sb[:], ktmp[:],
                start=True, stop=True),
                waits=((s_ac, kev), (s_ve, s_ve.n)), inc=s_te)
            pfence()
            pm_t = s_te.n
            P.on("vector", lambda e, par=par: e.tensor_mul(
                tmp1[:], ktmp[:], tabc[:, par * 512:(par + 1) * 512]),
                waits=((s_ac, kev), dma_t, (s_ve, s_ve.n)), inc=s_ve)
            P.on("vector", lambda e, par=par: e.tensor_mul(
                tmp2[:], o2[:, par * 512:(par + 1) * 512],
                tabs_[:, par * 512:(par + 1) * 512]),
                waits=((s_te, pm_t), (s_ve, s_ve.n)), inc=s_ve)
            P.on("vector", lambda e, dk=dk, ts=ts: e.tensor_add(
                krope[:, dk * T + ts * 512: dk * T + ts * 512 + 512],
                tmp1[:], tmp2[:]), waits=(s_ve,), inc=s_ve)
    krope_v = s_ve.n
    if DEBUG is True or DEBUG == "att1":
        P.on("gpsimd", lambda e: ld(e, dbg["krope"].ap(), krope[:]),
             waits=((s_ve, krope_v),), inc=s_sto, inc_k=16)

    P.on("sync", lambda e: ld(e, buvrow[:], _ap(b_uv, 0, [[0, 1], [1, 1024]])),
         waits=((s_ve, krope_v),), inc=s_dma, inc_k=16)
    buv_t = s_dma.n
    P.on("vector", lambda e: e.memset(
        _ap(vext, 64, [[NKV * 1040, 128], [1040, NKV], [65, 16]]), 1.0),
        waits=((s_ve, krope_v), (s_wob, s_wob.n), (s_kf, kf_done)), inc=s_ve)
    vones = s_ve.n
    prev_vev = [0]
    for tt in range(NT):
        w = [(s_ve, vones), (s_dma, buv_t), (s_ac, ckv_ac)]
        if prev_vev[0] > 0:
            w.append((s_ac, prev_vev[0]))
        for nh in range(2):
            P.on("tensor", lambda e, tt=tt, nh=nh: e.matmul(
                o2[:, nh * 512:(nh + 1) * 512], cv_sb[:, tt * 128:(tt + 1) * 128],
                wuv_sb[:, nh * 512:(nh + 1) * 512], start=True, stop=False),
                waits=tuple(w) if nh == 0 else (), inc=None)
            P.on("tensor", lambda e, nh=nh: e.matmul(
                o2[:, nh * 512:(nh + 1) * 512], onesf[0:1, :],
                buvrow[0:1, nh * 512:(nh + 1) * 512], start=False, stop=True),
                inc=(s_te if nh == 1 else None))
        pfence()
        P.on("scalar", lambda e, tt=tt: e.activation(
            _ap(vext, tt * 1040, [[NKV * 1040, 128], [65, 16], [1, 64]]),
            _ap(o2, 0, [[1024, 128], [64, 16], [1, 64]]),
            AF.Identity), waits=(s_te,), inc=s_ac)
        prev_vev[0] = s_ac.n
    v_ac = s_ac.n
    if DEBUG == "att1":
        P.on("gpsimd", lambda e: ld(e, dbg["buv"].ap(), buvrow[:]),
             waits=((s_ac, v_ac),), inc=s_sto, inc_k=16)
    if DEBUG is True or DEBUG == "att1":
        P.on("gpsimd", lambda e: ld(e, dbg["vext"].ap(), vext[:]),
             waits=((s_ac, v_ac),), inc=s_sto, inc_k=16)

    for q in range(NQ):
        for p_ in range(ND):
            qp = (q * ND + p_) % 2
            P.on("sync", lambda e, q=q, p_=p_, qp=qp: ld(
                e, qpair[:, qp * 512:(qp + 1) * 512],
                _ap(qrope, (p_ * 128) * T + q * 512, [[T, 128], [1, 512]])),
                waits=((s_qk, s_qk.n), (s_qwh[0], s_qwh[0].n),
                       (s_qwh[1], s_qwh[1].n)), inc=qpsm[qp], inc_k=16)
            qld = (qpsm[qp], qpsm[qp].n)
            def emit_exp(kv):
                P.on("scalar", lambda e, b=kv % 2, pj=kv % 3: e.activation(
                    p_sb[:, pj * 1024:(pj + 1) * 1024],
                    s4[:, b * 1024:(b + 1) * 1024], AF.Exp, scale=0.125),
                    waits=((s_qk, s_qk.n), (s_pv, s_pv.n)), inc=s_exp)

            for kv in range(NKV):
                b = kv % 2
                ws = [(s_exp, s_exp.n)]
                if kv == 0:
                    ws += [(s_ev, s_ev.n), qld, (s_ac, v_ac),
                           (s_ve, s_ve.n), (s_te, s_te.n)]
                for hh in range(2):
                    for cg in range(2):
                        krng = krope[hh * 64:(hh + 1) * 64,
                                     p_ * T + kv * 128 + cg * 64:
                                     p_ * T + kv * 128 + cg * 64 + 64]
                        qrng = qpair[hh * 64:(hh + 1) * 64,
                                     qp * 512:(qp + 1) * 512]
                        first = (hh == 0 and cg == 0)
                        last = (hh == 1 and cg == 1)
                        P.on("tensor", lambda e, b=b, hh=hh, cg=cg, krng=krng,
                             qrng=qrng: e.matmul(
                                 _ap(s4, cg * 64 * 2048 + b * 1024 + hh * 512,
                                     [[2048, 64], [1, 512]]),
                                 krng, qrng, start=True, stop=True,
                                 tile_position=(hh * 64, cg * 64)),
                             waits=tuple(ws) if first else (),
                             inc=(s_qk if last else None))
                if kv >= 1:
                    emit_exp(kv - 1)
                if kv >= 3:
                    emit_pv(P, kv - 3, p_, s4, o2, vext, p_sb, s_exp, s_pv,
                            lag=2)
            pfence(inc=s_qk)
            emit_exp(NKV - 1)
            emit_pv(P, NKV - 3, p_, s4, o2, vext, p_sb, s_exp, s_pv, lag=2)
            emit_pv(P, NKV - 2, p_, s4, o2, vext, p_sb, s_exp, s_pv, lag=1)
            emit_pv(P, NKV - 1, p_, s4, o2, vext, p_sb, s_exp, s_pv, lag=0)
            pfence(inc=s_pv)
            P.on("vector", lambda e: e.reciprocal(
                rr_sb[0:1, 0:512], _ap(o2, 64 * 1024, [[1024, 1], [1, 512]])),
                waits=((s_pv, s_pv.n), (s_ve, s_ve.n), (s_te, s_te.n)), inc=s_ve)
            P.on("vector", lambda e: e.reciprocal(
                rr_sb[0:1, 512:1024], _ap(o2, 64 * 1024 + 512, [[1024, 1], [1, 512]])),
                waits=(s_ve,), inc=s_ve)
            rcp = s_ve.n
            P.on("tensor", lambda e: e.matmul(
                aux[0:64, :], onesf[0:1, 0:64], rr_sb[0:1, 0:512],
                start=True, stop=True, tile_position=(0, 0)),
                waits=((s_ve, rcp),), inc=None)
            P.on("tensor", lambda e: e.matmul(
                aux[64:128, :], onesf[0:1, 0:64], rr_sb[0:1, 512:1024],
                start=True, stop=True, tile_position=(0, 64)), inc=s_te)
            pfence(out=s4[0:1, 0:1])
            rbc_t = s_te.n
            P.on("vector", lambda e: e.tensor_copy(rbc_sb[:], aux[:]),
                 waits=((s_te, rbc_t),), inc=s_ve)
            cp = s_ve.n
            P.on("vector", lambda e, p_=p_: e.tensor_mul(
                oc_sb[0:64, p_ * 512:(p_ + 1) * 512],
                _ap(o2, 0, [[1024, 64], [1, 512]]), rbc_sb[0:64, :]),
                waits=((s_ve, cp), (s_te, s_te.n)), inc=s_ve)
            P.on("vector", lambda e, p_=p_: e.tensor_mul(
                oc_sb[64:128, p_ * 512:(p_ + 1) * 512],
                _ap(o2, 512, [[1024, 64], [1, 512]]), rbc_sb[64:128, :]),
                waits=(s_ve,), inc=s_ev)
            emit_toep(4)
        oc_ready = s_ev.n
        for tp in range(2):
            for dk in range(ND):
                wp = dk % 2
                P.on("sync", lambda e, dk=dk, wp=wp: ld(
                    e, wob[:, wp * 1024:(wp + 1) * 1024],
                    _ap(wo_bf_d, dk * 128 * D, [[D, 128], [1, D]])),
                    waits=((s_wo, max(0, s_wo.n - 1)), (s_wob, s_wob.n)),
                    inc=wobsm[wp], inc_k=16)
                wo_ld = (wobsm[wp], wobsm[wp].n)
                for ts2 in range(2):
                    tsub = tp * 2 + ts2
                    for nh in range(2):
                        w = [wo_ld]
                        if dk == 0 and ts2 == 0 and nh == 0:
                            w.append((s_ev, oc_ready))
                            if tp == 1:
                                w.append((s_ve, s_ve.n))
                        last = (ts2 == 1 and nh == 1)
                        P.on("tensor", lambda e, dk=dk, tsub=tsub, nh=nh,
                             ts2=ts2, wp=wp: e.matmul(
                                 s4[:, ts2 * 1024 + nh * 512:
                                    ts2 * 1024 + nh * 512 + 512],
                                 oc_sb[:, dk * 512 + tsub * 128:
                                       dk * 512 + tsub * 128 + 128],
                                 wob[:, wp * 1024 + nh * 512:
                                     wp * 1024 + nh * 512 + 512],
                                 start=(dk == 0), stop=False),
                             waits=tuple(w), inc=(s_wo if last else None))
            for ts2 in range(2):
                for nh in range(2):
                    P.on("tensor", lambda e, nh=nh, ts2=ts2: e.matmul(
                        s4[:, ts2 * 1024 + nh * 512: ts2 * 1024 + nh * 512 + 512],
                        onesf[0:1, :], borow[0:1, nh * 512:(nh + 1) * 512],
                        start=False, stop=True), inc=(s_te if nh == 1 else None))
            pfence()
            wo_mm = s_te.n
            for ts2 in range(2):
                tsub = tp * 2 + ts2
                ti = q * 4 + tsub
                P.on("sync", lambda e, ti=ti, ts2=ts2: ld(
                    e, xtile[:, ts2 * 1024:(ts2 + 1) * 1024],
                    x_in.ap()[ti * 128:(ti + 1) * 128, :]),
                    waits=((s_ve, s_ve.n),), inc=lnx[ts2], inc_k=16)
                x_ld = (lnx[ts2], lnx[ts2].n)
                P.on("vector", lambda e, ts2=ts2: e.tensor_add(
                    x2stg[:, ts2 * 1024:(ts2 + 1) * 1024],
                    s4[:, ts2 * 1024:ts2 * 1024 + 1024],
                    xtile[:, ts2 * 1024:(ts2 + 1) * 1024]),
                    waits=((s_te, wo_mm), x_ld, (s_r1h[ts2], s_r1h[ts2].n)),
                    inc=s_ve)
                st_t = s_ve.n
                P.on("gpsimd", lambda e, ti=ti, ts2=ts2: ld(
                    e, r1.ap()[ti * 128:(ti + 1) * 128, :],
                    x2stg[:, ts2 * 1024:(ts2 + 1) * 1024]),
                    waits=((s_ve, st_t),), inc=s_r1h[ts2], inc_k=16)

    emit_toep(len(toep_dmas))

    wfft = A.at("wfft", 16384, [128, 8 * 1024], BF16)
    fxt = A.at("fxt", 32768, [128, 2 * 1024], F32)
    fsq = A.at("fsq", 40960, [128, 2 * 1024], F32)
    fxn = A.at("fxn", 49152, [128, 2 * 1024], F32)
    ftr = A.at("ftr", 57344, [128, 2 * 1024], F32)
    fxnc = A.at("fxnc", 65536, [128, 2 * 4096], F32)
    fxq = A.at("fxq", 98304, [128, 2 * 4096], BF16)
    ffc = A.at("ffc", 114688, [128, 2 * 4096], F32)
    fx2 = A.at("fx2", 147456, [128, 2 * 1024], F32)
    fx3 = A.at("fx3", 155648, [128, 2 * 1024], F32)
    w8stg = A.at("w8stg", 163840, [128, 8 * 1024], I8)
    w8bf = A.at("w8bf", 172032, [128, 1024], BF16)
    fym = A.at("fym", 174080, [128, 4096], F32)
    fevt = A.at("fevt", 190464, [128, 2 * 512], F32)
    ccin = dt_("ccin", [1, 1], F32)
    ccout = dt_("ccout", [1, 1], F32)
    wfft_built = [0]

    def emit_wfft():
        P.on("sync", lambda e: ld(
            e, w8stg[:], _ap(ffn_w8, 0, [[1024, 128], [128 * 1024, 8], [1, 1024]])),
            waits=((s_ve, s_ve.n), (s_te, s_te.n)), inc=kstsm, inc_k=16)
        wld = (kstsm, kstsm.n)
        for ncb in range(8):
            P.on("vector", lambda e, ncb=ncb: e.tensor_copy(
                w8bf[:], w8stg[:, ncb * 1024:(ncb + 1) * 1024]),
                waits=(wld, (s_ve, s_ve.n), (s_te, s_te.n)), inc=s_ve)
            cst = s_ve.n
            P.on("tensor", lambda e: [e.transpose(
                pb[:, kc * 128:(kc + 1) * 128], w8bf[:, kc * 128:(kc + 1) * 128],
                ident_sb[:]) for kc in range(8)][-1],
                waits=((s_ve, cst),), inc=None)
            tfence()
            P.on("vector", lambda e, ncb=ncb: e.tensor_copy(
                _ap(wfft, ncb * 128, [[8 * 1024, 128], [1024, 8], [1, 128]]),
                pb[:]), waits=(s_te,), inc=s_ve)
        wfft_built[0] = s_ve.n

    def emit_ffn(src, dst, xnc, final, src_w=()):
        f_cc = P.sem()
        cons = [0, 0]
        sqr = [0, 0]
        ev_prev = [0]
        tp_prev = [0]
        for i in range(NT):
            par = i % 2
            sm = i % 2
            w = ((s_ve, cons[par]),) if i >= 2 else (
                (s_ve, s_ve.n), (s_te, s_te.n), (s_ac, s_ac.n)) + tuple(src_w)
            P.on("sync", lambda e, i=i, par=par: ld(
                e, fxt[:, par * 1024:(par + 1) * 1024],
                src.ap()[i * 128:(i + 1) * 128, :]), waits=w, inc=lnx[par], inc_k=16)
            xld = (lnx[par], lnx[par].n)
            xt = fxt[:, par * 1024:(par + 1) * 1024]
            P.on("vector", lambda e, xt=xt, sm=sm: e.reduce_sum(
                sml[:, sm:sm + 1], xt, axis=AX), waits=(xld,), inc=s_ve)
            P.on("scalar", lambda e, xt=xt, par=par: e.activation(
                fsq[:, par * 1024:(par + 1) * 1024], xt, AF.Square),
                waits=(xld, (s_ve, sqr[par])), inc=s_ac)
            P.on("vector", lambda e, sm=sm, par=par: e.reduce_sum(
                sml[:, 2 + sm:3 + sm], fsq[:, par * 1024:(par + 1) * 1024], axis=AX),
                waits=(s_ac,), inc=s_ve)
            sqr[par] = s_ve.n
            P.on("vector", lambda e, sm=sm: e.tensor_scalar(
                sml[:, 4 + sm:5 + sm], sml[:, sm:sm + 1], -1.0 / D, None, ALU.mult),
                waits=(s_ve,), inc=s_ve)
            P.on("vector", lambda e, sm=sm: e.tensor_mul(
                sml[:, 6 + sm:7 + sm], sml[:, sm:sm + 1], sml[:, sm:sm + 1]),
                waits=(s_ve,), inc=s_ve)
            P.on("vector", lambda e, sm=sm: e.tensor_scalar(
                sml[:, 8 + sm:9 + sm], sml[:, 2 + sm:3 + sm], 1.0 / D, EPS,
                ALU.mult, ALU.add), waits=(s_ve,), inc=s_ve)
            P.on("vector", lambda e, sm=sm: e.scalar_tensor_tensor(
                sml[:, 10 + sm:11 + sm], sml[:, 6 + sm:7 + sm], -1.0 / (D * D),
                sml[:, 8 + sm:9 + sm], ALU.mult, ALU.add), waits=(s_ve,), inc=s_ve)
            P.on("scalar", lambda e, sm=sm: e.activation(
                sml[:, 12 + sm:13 + sm], sml[:, 10 + sm:11 + sm], AF.Ln),
                waits=(s_ve,), inc=s_ac)
            P.on("scalar", lambda e, sm=sm: e.activation(
                sml[:, 14 + sm:15 + sm], sml[:, 12 + sm:13 + sm], AF.Exp, scale=-0.5),
                waits=(s_ac,), inc=s_ac)
            P.on("vector", lambda e, xt=xt, sm=sm, par=par: e.tensor_scalar(
                fxn[:, par * 1024:(par + 1) * 1024], xt,
                sml[:, 4 + sm:5 + sm], sml[:, 14 + sm:15 + sm], ALU.add, ALU.mult),
                waits=((s_ac, s_ac.n), (s_te, tp_prev[0])), inc=s_ve)
            nrm = s_ve.n
            cons[par] = s_ve.n
            P.on("vector", lambda e, i=i, par=par: e.reduce_max(
                maxs_sb[:, i:i + 1], fxn[:, par * 1024:(par + 1) * 1024], axis=AX,
                apply_absolute_value=True), waits=((s_ve, nrm),), inc=s_ve)
            P.on("tensor", lambda e, par=par: [e.transpose(
                _ap(o2, k * 128, [[1024, 128], [1, 128]]),
                fxn[:, par * 1024 + k * 128: par * 1024 + (k + 1) * 128],
                identf_sb[:]) for k in range(8)][-1],
                waits=((s_ve, s_ve.n), (s_ve, ev_prev[0])), inc=None)
            tfence()
            tp_prev[0] = s_te.n
            P.on("vector", lambda e, par=par: e.tensor_copy(
                ftr[:, par * 1024:(par + 1) * 1024], o2[:]),
                waits=(s_te, (s_xch[par], s_xch[par].n)), inc=s_ve)
            ev_prev[0] = s_ve.n
            P.on("gpsimd", lambda e, i=i, par=par: ld(
                e, _ap(xnc, i * 128, [[T, 128], [128 * T, 8], [1, 128]]),
                ftr[:, par * 1024:(par + 1) * 1024]),
                waits=((s_ve, ev_prev[0]),), inc=s_xch[par], inc_k=16)
        ln_done = s_ve.n
        P.on("vector", lambda e: e.reduce_max(
            sml[:, 16:17], maxs_sb[:], axis=AX), waits=((s_ve, ln_done),), inc=s_ve)
        P.on("vector", lambda e: e.memset(ftr[:, 0:128], 0.0),
             waits=((s_xch[0], s_xch[0].n), (s_xch[1], s_xch[1].n)), inc=s_ve)
        P.on("vector", lambda e: e.tensor_copy(ftr[:, 0:1], sml[:, 16:17]),
             waits=(s_ve,), inc=s_ve)
        mx = s_ve.n
        P.on("tensor", lambda e: e.transpose(
            _ap(o2, 0, [[1024, 128], [1, 128]]), ftr[:, 0:128], identf_sb[:]),
            waits=((s_ve, mx), (s_ve, s_ve.n)), inc=None)
        tfence()
        P.on("vector", lambda e: e.reduce_max(
            sml[0:1, 17:18], _ap(o2, 0, [[1024, 1], [1, 128]]), axis=AX),
            waits=(s_te,), inc=s_ve)
        P.on("vector", lambda e: e.tensor_copy(ftr[0:1, 0:1], sml[0:1, 17:18]),
             waits=(s_ve,), inc=s_ve)
        P.on("gpsimd", lambda e: ld(e, ccin.ap(), ftr[0:1, 0:1]),
             waits=((s_ve, s_ve.n),), inc=s_ccio, inc_k=16)
        P.on("gpsimd", lambda e: e.collective_compute(
            "AllReduce", ALU.max, replica_groups=[list(range(8))],
            ins=[ccin.ap()], outs=[ccout.ap()]),
            waits=((s_ccio, s_ccio.n),), inc=f_cc)
        P.on("gpsimd", lambda e: ld(e, ftr[0:1, 0:1], ccout.ap()),
             waits=((f_cc, 1),), inc=s_ccio, inc_k=16)
        ccd = s_ccio.n
        P.on("vector", lambda e: e.tensor_scalar(
            sml[0:1, 18:19], ftr[0:1, 0:1], 1.0 / 127.0, EPS, ALU.mult, ALU.max),
            waits=((s_ccio, ccd),), inc=s_ve)
        sc = s_ve.n
        if DEBUG in ("sc", "att1") or DEBUG is True:
            fi = 0 if src is r1 else 1
            for j, sl in ((0, sml[0:1, 17:18]), (1, ftr[0:1, 0:1]),
                          (2, sml[0:1, 18:19])):
                P.on("gpsimd", lambda e, fi=fi, j=j, sl=sl: ld(
                    e, _ap(dbg["sc"], fi * 4 + j, [[8, 1], [1, 1]]), sl),
                    waits=((s_ve, sc),), inc=s_scd, inc_k=16)
        P.on("tensor", lambda e: e.matmul(
            _ap(aux, 0, [[512, 128], [1, 1]]), onesf[0:1, :], sml[0:1, 18:19],
            start=True, stop=True), waits=((s_ve, sc),), inc=s_te)
        pfence(out=_ap(aux, 1, [[512, 1], [1, 1]]))
        P.on("vector", lambda e: e.tensor_copy(
            sml[:, 20:21], _ap(aux, 0, [[512, 128], [1, 1]])),
            waits=(s_te,), inc=s_ve)
        P.on("vector", lambda e: e.reciprocal(invs_sb[:], sml[:, 20:21]),
             waits=(s_ve,), inc=s_ve)
        P.on("vector", lambda e: e.tensor_scalar(
            ssw_sb[:], wsc_sb[:], sml[:, 20:21], None, ALU.mult),
            waits=(s_ve,), inc=s_ve)
        scales_v = s_ve.n
        bank_fr = [0, 0, 0, 0]
        evtf = [0, 0]
        qprev = [0, 0]
        qacc = [0, 0]
        ymf = [0]
        addp = [0]
        for ts in range(NQ):
            par = ts % 2
            w = [(s_xch[0], s_xch[0].n), (s_xch[1], s_xch[1].n)]
            if qacc[par] > 0:
                w.append((s_ac, qacc[par]))
            P.on("sync", lambda e, ts=ts, par=par: ld(
                e, fxnc[:, par * 4096:(par + 1) * 4096],
                _ap(xnc, ts * 512, [[T, 128], [128 * T, 8], [1, 512]])),
                waits=tuple(w), inc=fxnsm[par], inc_k=16)
            cld = (fxnsm[par], fxnsm[par].n)
            P.on("scalar", lambda e, par=par: e.activation(
                fym[:], fxnc[:, par * 4096:(par + 1) * 4096], AF.Identity,
                bias=mgc_sb[:], scale=invs_sb[:]),
                waits=(cld, (s_ve, scales_v), (s_ve, ymf[0])), inc=s_ac)
            qacc[par] = s_ac.n
            P.on("vector", lambda e, par=par: e.tensor_scalar(
                fxq[:, par * 4096:(par + 1) * 4096], fym[:], -MAGIC, None, ALU.add),
                waits=((s_ac, s_ac.n), (s_te, qprev[par])), inc=s_ve)
            ymf[0] = s_ve.n
            qrdy = s_ve.n
            for ncb in range(8):
                bank = ncb % 4
                for kc in range(8):
                    w = [(s_ve, qrdy), (s_ve, wfft_built[0])]
                    if kc == 0 and bank_fr[bank] > 0:
                        w.append((s_ac, bank_fr[bank]))
                    P.on("tensor", lambda e, ncb=ncb, kc=kc, bank=bank, par=par:
                         e.matmul(
                             s4[:, bank * 512:(bank + 1) * 512],
                             wfft[:, kc * 1024 + ncb * 128: kc * 1024 + ncb * 128 + 128],
                             fxq[:, par * 4096 + kc * 512: par * 4096 + (kc + 1) * 512],
                             start=(kc == 0), stop=(kc == 7)),
                         waits=tuple(w) if kc == 0 else (),
                         inc=(s_te if kc == 7 else None))
                ep = ncb % 2
                pfence()
                P.on("scalar", lambda e, bank=bank, ep=ep: e.activation(
                    fevt[:, ep * 512:(ep + 1) * 512],
                    s4[:, bank * 512:(bank + 1) * 512], AF.Identity),
                    waits=((s_te, s_te.n), (s_ve, evtf[ep])), inc=s_ac)
                bank_fr[bank] = s_ac.n
                P.on("vector", lambda e, ncb=ncb, par=par, ep=ep: e.tensor_scalar(
                    ffc[:, par * 4096 + ncb * 512: par * 4096 + (ncb + 1) * 512],
                    fevt[:, ep * 512:(ep + 1) * 512],
                    ssw_sb[:, ncb:ncb + 1], bff_sb[:, ncb:ncb + 1],
                    ALU.mult, ALU.add),
                    waits=((s_ac, s_ac.n),), inc=s_ve)
                evtf[ep] = s_ve.n
            qprev[par] = s_te.n
            evd_v = s_ve.n
            for tsub in range(4):
                sp = tsub % 2
                ti = ts * 4 + tsub
                P.on("sync", lambda e, ti=ti, sp=sp: ld(
                    e, fx2[:, sp * 1024:(sp + 1) * 1024],
                    src.ap()[ti * 128:(ti + 1) * 128, :]),
                    waits=((s_ve, addp[0]),), inc=fx2sm[sp], inc_k=16)
                xld2 = (fx2sm[sp], fx2sm[sp].n)
                P.on("tensor", lambda e, tsub=tsub, par=par: [e.transpose(
                    _ap(o2, k * 128, [[1024, 128], [1, 128]]),
                    ffc[:, par * 4096 + k * 512 + tsub * 128:
                        par * 4096 + k * 512 + tsub * 128 + 128],
                    identf_sb[:]) for k in range(8)][-1],
                    waits=((s_ve, evd_v), (s_ve, addp[0])), inc=None)
                tfence()
                trd = s_te.n
                P.on("vector", lambda e, sp=sp: e.tensor_add(
                    fx3[:, sp * 1024:(sp + 1) * 1024], o2[:],
                    fx2[:, sp * 1024:(sp + 1) * 1024]),
                    waits=((s_te, trd), xld2, (s_dsth[sp], s_dsth[sp].n)), inc=s_ve)
                addp[0] = s_ve.n
                P.on("gpsimd", lambda e, ti=ti, sp=sp: ld(
                    e, dst.ap()[ti * 128:(ti + 1) * 128, :],
                    fx3[:, sp * 1024:(sp + 1) * 1024]),
                    waits=((s_ve, addp[0]),), inc=s_dsth[sp], inc_k=16)
        if final:
            fw = ((s_dsth[0], s_dsth[0].n), (s_dsth[1], s_dsth[1].n))
            if DEBUG in ("sc", "att1") or DEBUG is True:
                fw = fw + ((s_scd, s_scd.n),)
            P.on("sync", lambda e: None, waits=fw)

    xn2c = dt_("xn2c", [ND, 128, T], F32)
    if (DEBUG in ("sc", "att1") or DEBUG is True) and "ffn1" in PHASES:
        P.on("gpsimd", lambda e: ld(e, dbg["r1"].ap(), r1.ap()),
             waits=((s_r1h[0], s_r1h[0].n), (s_r1h[1], s_r1h[1].n)),
             inc=s_scd, inc_k=16)
    if "ffn1" in PHASES:
        emit_wfft()
        emit_ffn(r1, r2 if "attnb" in PHASES else out_t, xn2c,
                 final=("attnb" not in PHASES),
                 src_w=((s_r1h[0], s_r1h[0].n), (s_r1h[1], s_r1h[1].n)))

    winp = A.at("winp", 32768, [128, 8 * 1024], BF16)
    v_t = A.at("v_t", 49152, [128, 1024 * 33], BF16)
    x1_t = A.at("x1_t", 116736, [128, 1024 * 33], BF16)
    bxnbb = A.at("bxnbb", 184320, [128, 2 * 1024], BF16)
    tpb = A.at("tpb", 188416, [128, 8 * 256], BF16)
    binrow = A.at("binrow", 192512, [1, 1024], F32)
    cvsf = A.at("cvsf", 192512, [128, 512], F32)
    cvs2 = A.at("cvs2", 194560, [128, 512], BF16)
    bxnb = A.at("bxnb", 98304, [128, 1024], BF16)
    bxnbt = A.at("bxnbt", 100352, [128, 1024], BF16)
    wout_sb = winp
    y2c = A.at("y2c", 116736, [128, 2 * 4096], BF16)
    woc = A.at("woc", 133120, [128, 2 * 4096], F32)
    z3rd = A.at("z3rd", 165888, [128, 2 * 1024], F32)
    x4st = A.at("x4st", 174080, [128, 2 * 1024], F32)
    bout_row = A.at("bout_row", 184320, [1, D], F32)
    ones512b = A.at("ones512b", 188416, [1, 512], F32)

    def emit_attnb(src, dst, final, src_w=()):
        b_cc = P.sem()
        cons = [0, 0]
        sqr = [0, 0]
        tp_prev = [0]
        for i in range(NT):
            par = i % 2
            sm = i % 2
            w = ((s_ve, cons[par]),) if i >= 2 else (
                (s_ve, s_ve.n), (s_te, s_te.n), (s_ac, s_ac.n)) + tuple(src_w)
            P.on("sync", lambda e, i=i, par=par: ld(
                e, fxt[:, par * 1024:(par + 1) * 1024],
                src.ap()[i * 128:(i + 1) * 128, :]), waits=w, inc=lnx[par], inc_k=16)
            xld = (lnx[par], lnx[par].n)
            xt = fxt[:, par * 1024:(par + 1) * 1024]
            P.on("vector", lambda e, xt=xt, sm=sm: e.reduce_sum(
                sml[:, sm:sm + 1], xt, axis=AX), waits=(xld,), inc=s_ve)
            P.on("scalar", lambda e, xt=xt, par=par: e.activation(
                fsq[:, par * 1024:(par + 1) * 1024], xt, AF.Square),
                waits=(xld, (s_ve, sqr[par])), inc=s_ac)
            P.on("vector", lambda e, sm=sm, par=par: e.reduce_sum(
                sml[:, 2 + sm:3 + sm], fsq[:, par * 1024:(par + 1) * 1024], axis=AX),
                waits=(s_ac,), inc=s_ve)
            sqr[par] = s_ve.n
            P.on("vector", lambda e, sm=sm: e.tensor_scalar(
                sml[:, 4 + sm:5 + sm], sml[:, sm:sm + 1], -1.0 / D, None, ALU.mult),
                waits=(s_ve,), inc=s_ve)
            P.on("vector", lambda e, sm=sm: e.tensor_mul(
                sml[:, 6 + sm:7 + sm], sml[:, sm:sm + 1], sml[:, sm:sm + 1]),
                waits=(s_ve,), inc=s_ve)
            P.on("vector", lambda e, sm=sm: e.tensor_scalar(
                sml[:, 8 + sm:9 + sm], sml[:, 2 + sm:3 + sm], 1.0 / D, EPS,
                ALU.mult, ALU.add), waits=(s_ve,), inc=s_ve)
            P.on("vector", lambda e, sm=sm: e.scalar_tensor_tensor(
                sml[:, 10 + sm:11 + sm], sml[:, 6 + sm:7 + sm], -1.0 / (D * D),
                sml[:, 8 + sm:9 + sm], ALU.mult, ALU.add), waits=(s_ve,), inc=s_ve)
            P.on("scalar", lambda e, sm=sm: e.activation(
                sml[:, 12 + sm:13 + sm], sml[:, 10 + sm:11 + sm], AF.Ln),
                waits=(s_ve,), inc=s_ac)
            P.on("scalar", lambda e, sm=sm: e.activation(
                sml[:, 14 + sm:15 + sm], sml[:, 12 + sm:13 + sm], AF.Exp, scale=-0.5),
                waits=(s_ac,), inc=s_ac)
            P.on("vector", lambda e, xt=xt, sm=sm, par=par: e.tensor_scalar(
                fxn[:, par * 1024:(par + 1) * 1024], xt,
                sml[:, 4 + sm:5 + sm], sml[:, 14 + sm:15 + sm], ALU.add, ALU.mult),
                waits=((s_ac, s_ac.n),), inc=s_ve)
            nrm = s_ve.n
            xnt = fxn[:, par * 1024:(par + 1) * 1024]
            P.on("vector", lambda e, xt=xt, xnt=xnt, par=par: e.tensor_add(
                fx3[:, par * 1024:(par + 1) * 1024], xnt, xt),
                waits=((s_ve, nrm), (s_z3h[par], s_z3h[par].n)), inc=s_ve)
            cons[par] = s_ve.n
            P.on("gpsimd", lambda e, i=i, par=par: ld(
                e, z3.ap()[i * 128:(i + 1) * 128, :],
                fx3[:, par * 1024:(par + 1) * 1024]),
                waits=((s_ve, s_ve.n),), inc=s_z3h[par], inc_k=16)
            P.on("vector", lambda e, xnt=xnt, sm=sm: e.reduce_sum(
                sml[:, 16 + sm:17 + sm], xnt, axis=AX), waits=((s_ve, nrm),), inc=s_ve)
            P.on("vector", lambda e, xnt=xnt, sm=sm: e.scalar_tensor_tensor(
                ftr[:, (sm) * 1024:(sm + 1) * 1024], xnt, 1.0, xnt,
                ALU.mult, ALU.mult, accum_out=sml[:, 18 + sm:19 + sm]),
                waits=((s_ve, s_ve.n), (s_ac, s_ac.n)), inc=s_ve)
            P.on("vector", lambda e, sm=sm: e.tensor_scalar(
                sml[:, 4 + sm:5 + sm], sml[:, 16 + sm:17 + sm], -1.0 / D, None,
                ALU.mult), waits=(s_ve,), inc=s_ve)
            P.on("vector", lambda e, sm=sm: e.tensor_mul(
                sml[:, 6 + sm:7 + sm], sml[:, 16 + sm:17 + sm],
                sml[:, 16 + sm:17 + sm]), waits=(s_ve,), inc=s_ve)
            P.on("vector", lambda e, sm=sm: e.tensor_scalar(
                sml[:, 8 + sm:9 + sm], sml[:, 18 + sm:19 + sm], 1.0 / D, EPS,
                ALU.mult, ALU.add), waits=(s_ve,), inc=s_ve)
            P.on("vector", lambda e, sm=sm: e.scalar_tensor_tensor(
                sml[:, 10 + sm:11 + sm], sml[:, 6 + sm:7 + sm], -1.0 / (D * D),
                sml[:, 8 + sm:9 + sm], ALU.mult, ALU.add), waits=(s_ve,), inc=s_ve)
            P.on("scalar", lambda e, sm=sm: e.activation(
                sml[:, 12 + sm:13 + sm], sml[:, 10 + sm:11 + sm], AF.Ln),
                waits=(s_ve,), inc=s_ac)
            P.on("scalar", lambda e, sm=sm: e.activation(
                sml[:, 14 + sm:15 + sm], sml[:, 12 + sm:13 + sm], AF.Exp,
                scale=-0.5), waits=(s_ac,), inc=s_ac)
            P.on("vector", lambda e, xnt=xnt, sm=sm: e.tensor_scalar(
                bxnb[:], xnt, sml[:, 4 + sm:5 + sm], sml[:, 14 + sm:15 + sm],
                ALU.add, ALU.mult),
                waits=((s_ac, s_ac.n), (s_te, tp_prev[0])), inc=s_ve)
            nb = s_ve.n
            P.on("tensor", lambda e: [e.transpose(
                pb[:, k * 128:(k + 1) * 128], bxnb[:, k * 128:(k + 1) * 128],
                ident_sb[:]) for k in range(8)][-1],
                waits=((s_ve, nb), (s_ve, s_ve.n)), inc=None)
            tfence()
            tp_prev[0] = s_te.n
            P.on("vector", lambda e: e.tensor_copy(bxnbt[:], pb[:]),
                 waits=(s_te, (s_xbst, s_xbst.n)), inc=s_ve)
            bev = s_ve.n
            P.on("gpsimd", lambda e, i=i: ld(
                e, _ap(xnbc, i * 128 * (ND * 128), [[ND * 128, 128], [1, ND * 128]]),
                bxnbt[:]), waits=((s_ve, bev),), inc=s_xbst, inc_k=16)
        ln3_done = s_xbst.n

        wpstg = A.at("wpstg", 133120, [128, 8 * 1024], F32)

        def w_part(part, dst_t, gate_conv):
            poff = part * 1024
            P.on("sync", lambda e: ld(
                e, binrow[:], _ap(b_in, poff, [[3072, 1], [1, 1024]])),
                waits=((s_ve, s_ve.n), (s_te, s_te.n)), inc=s_dma, inc_k=16)
            bld = s_dma.n
            P.on("sync", lambda e: ld(
                e, wpstg[:], _ap(W_in, poff, [[3072, 128], [128 * 3072, 8],
                                              [1, 1024]])),
                waits=((s_ac, s_ac.n), (s_te, s_te.n)), inc=wstsm, inc_k=16)
            wld = (wstsm, wstsm.n)
            for dk in range(ND):
                P.on("scalar", lambda e, dk=dk: e.activation(
                    winp[:, dk * 1024:(dk + 1) * 1024],
                    wpstg[:, dk * 1024:(dk + 1) * 1024],
                    AF.Identity, scale=gam_sb[:, dk:dk + 1]),
                    waits=(wld,), inc=s_ac)
            bwv = [0]
            for cp in range(4):
                for dk in range(ND):
                    aw = [wld]
                    if dk == 0 and bwv[0] > 0:
                        aw.append((s_ve, bwv[0]))
                    P.on("tensor", lambda e, cp=cp, dk=dk: e.matmul(
                        _ap(aux, (cp % 2) * 256, [[512, 1], [1, 256]]),
                        beta_sb[:, dk:dk + 1],
                        wpstg[:, dk * 1024 + cp * 256: dk * 1024 + cp * 256 + 256],
                        start=(dk == 0), stop=(dk == ND - 1)),
                        waits=tuple(aw) if dk == 0 else (),
                        inc=(s_te if dk == ND - 1 else None))
                pfence(out=s4[0:1, 1024:1025])
                P.on("vector", lambda e, cp=cp: e.tensor_add(
                    binrow[0:1, cp * 256:(cp + 1) * 256],
                    binrow[0:1, cp * 256:(cp + 1) * 256],
                    _ap(aux, (cp % 2) * 256, [[512, 1], [1, 256]])),
                    waits=((s_te, s_te.n), (s_dma, bld)), inc=s_ve)
                bwv[0] = s_ve.n
            wrdy_ac = s_ac.n
            wrdy_ve = s_ve.n
            ev_pr = [0, 0]
            xb_pr = [0, 0]
            for i in range(NT):
                par = i % 2
                w = [(s_xbst, ln3_done)]
                if xb_pr[par] > 0:
                    w.append((s_te, xb_pr[par]))
                P.on("sync", lambda e, i=i, par=par: ld(
                    e, bxnbb[:, par * 1024:(par + 1) * 1024],
                    _ap(xnbc, i * 128 * (ND * 128), [[ND * 128, 128], [1, ND * 128]])),
                    waits=tuple(w), inc=bxbsm[par], inc_k=16)
                xbl = (bxbsm[par], bxbsm[par].n)
                for nc2 in range(2):
                    wm = [xbl, (s_ac, wrdy_ac), (s_ve, wrdy_ve)]
                    if ev_pr[nc2] > 0:
                        wm.append((s_ve, ev_pr[nc2]))
                    for dk in range(ND):
                        P.on("tensor", lambda e, i=i, par=par, nc2=nc2, dk=dk:
                             e.matmul(
                                 s4[:, nc2 * 512:(nc2 + 1) * 512],
                                 bxnbb[:, par * 1024 + dk * 128:
                                       par * 1024 + (dk + 1) * 128],
                                 winp[:, dk * 1024 + nc2 * 512:
                                      dk * 1024 + nc2 * 512 + 512],
                                 start=(dk == 0), stop=False),
                             waits=tuple(wm) if dk == 0 else (), inc=None)
                    P.on("tensor", lambda e, nc2=nc2: e.matmul(
                        s4[:, nc2 * 512:(nc2 + 1) * 512], onesf[0:1, :],
                        binrow[0:1, nc2 * 512:(nc2 + 1) * 512],
                        start=False, stop=True), inc=s_te)
                    pfence(out=s4[0:1, 1024:1025])
                    if nc2 == 1:
                        xb_pr[par] = s_te.n
                    mmv = s_te.n
                    P.on("vector", lambda e, i=i, nc2=nc2, dst_t=dst_t:
                         e.tensor_copy(
                             _ap(dst_t, (nc2 * 512) * 33 + i + 1,
                                 [[1024 * 33, 128], [33, 512]]),
                             s4[:, nc2 * 512:(nc2 + 1) * 512]),
                         waits=((s_te, mmv),), inc=s_ve)
                    ev_pr[nc2] = s_ve.n
            return s_ve.n

        P.on("vector", lambda e: e.memset(
            _ap(v_t, 0, [[1024 * 33, 128], [33, 1024]]), 0.0),
            waits=((s_ve, s_ve.n), (s_te, s_te.n)), inc=s_ve)
        if DEBUG == "attnb":
            P.on("gpsimd", lambda e: ld(e, dbg["xnbc"].ap(), xnbc.ap()),
                 waits=((s_xbst, ln3_done),), inc=s_sto, inc_k=16)
            P.on("gpsimd", lambda e: ld(e, dbg["z3"].ap(), z3.ap()),
                 waits=((s_xbst, ln3_done),), inc=s_sto, inc_k=16)
        w_part(0, v_t, None)
        if DEBUG == "attnb":
            P.on("gpsimd", lambda e: ld(e, dbg["vt"].ap(), v_t[:]),
                 waits=((s_ve, s_ve.n),), inc=s_sto, inc_k=16)
        w_part(1, x1_t, None)
        if DEBUG == "attnb":
            P.on("gpsimd", lambda e: ld(e, dbg["x1t"].ap(), x1_t[:]),
                 waits=((s_ve, s_ve.n),), inc=s_sto, inc_k=16)

        def conv_pass(order, ysrc, gsrc, toep_off):
            tp_ld = [0] * 8
            tp_use = [0] * 8
            evA = [0]
            evB = [s_ve.n]
            gdone = [s_ve.n]
            cvs_te = [0]
            for g16 in range(64):
                for rnd in range(4):
                    for cc in range(4):
                        ch = g16 * 16 + rnd * 4 + cc
                        slot = ch % 8
                        w = []
                        if tp_use[slot] > 0:
                            w.append((s_te, tp_use[slot]))
                        if g16 == 0 and rnd == 0 and cc == 0:
                            w.append((s_tp, s_tp.n))
                        P.on("sync", lambda e, ch=ch, slot=slot: ld(
                            e, tpb[:, slot * 256:(slot + 1) * 256],
                            _ap(toep, (toep_off + ch) * 128 * 256,
                                [[256, 128], [1, 256]])),
                            waits=tuple(w), inc=tpsm[slot], inc_k=16)
                        tp_ld[slot] = (tpsm[slot], tpsm[slot].n)
                        wm = [tp_ld[slot]]
                        if rnd == 0 and cc == 0:
                            wm.append((s_ve, evB[0]))
                            if evA[0] > 0:
                                wm.append((s_ac, evA[0]))
                        P.on("tensor", lambda e, ch=ch, slot=slot, rnd=rnd, cc=cc,
                             ysrc=ysrc: e.matmul(
                                 _ap(o2, (cc * 32) * 1024 + rnd * 128,
                                     [[1024, 32], [1, 128]]),
                                 _ap(ysrc, ch * 33 + 1, [[1024 * 33, 128], [1, 32]]),
                                 tpb[:, slot * 256: slot * 256 + 128],
                                 start=True, stop=True,
                                 tile_position=(0, cc * 32)),
                             waits=tuple(wm), inc=None)
                        P.on("tensor", lambda e, ch=ch, slot=slot, rnd=rnd, cc=cc,
                             ysrc=ysrc: e.matmul(
                                 _ap(o2, (cc * 32) * 1024 + 512 + rnd * 128,
                                     [[1024, 32], [1, 128]]),
                                 _ap(ysrc, ch * 33, [[1024 * 33, 128], [1, 32]]),
                                 tpb[:, slot * 256 + 128: slot * 256 + 256],
                                 start=True, stop=True,
                                 tile_position=(0, cc * 32)),
                             inc=s_te)
                        tp_use[slot] = s_te.n
                        if CONV_SERIAL:
                            P.on("tensor", lambda e: None,
                                 waits=((s_te, s_te.n),))
                pfence(out=s4[0:1, 0:1])
                mm16 = s_te.n
                P.on("scalar", lambda e: e.activation(
                    cvsf[:], o2[:, 0:512], AF.Identity),
                    waits=((s_te, mm16), (s_te, cvs_te[0]), (s_ve, s_ve.n)), inc=s_ac)
                evA[0] = s_ac.n
                P.on("vector", lambda e: e.tensor_add(
                    cvs2[:], cvsf[:], o2[:, 512:1024]),
                    waits=((s_ac, evA[0]), (s_te, cvs_te[0])), inc=s_ve)
                evB[0] = s_ve.n
                P.on("tensor", lambda e: [e.transpose(
                    pb[:, r * 128:(r + 1) * 128], cvs2[:, r * 128:(r + 1) * 128],
                    ident_sb[:]) for r in range(4)][-1],
                    waits=((s_ve, evB[0]), (s_ve, gdone[0])), inc=None)
                tfence()
                cvs_te[0] = s_te.n
                P.on("vector", lambda e, g16=g16, gsrc=gsrc, ysrc=ysrc: e.tensor_mul(
                    _ap(ysrc, (g16 * 16) * 33 + 1,
                        [[1024 * 33, 128], [132, 4], [33, 4], [1, 32]]),
                    _ap(pb, 0, [[1024, 128], [128, 4], [32, 4], [1, 32]]),
                    _ap(gsrc, (g16 * 16) * 33 + 1,
                        [[1024 * 33, 128], [132, 4], [33, 4], [1, 32]])),
                    waits=((s_te, cvs_te[0]),), inc=s_ve)
                gdone[0] = s_ve.n
            return s_ve.n

        conv_pass(0, v_t, x1_t, 0)
        if DEBUG == "attnb":
            P.on("gpsimd", lambda e: ld(e, dbg["y1"].ap(), v_t[:]),
                 waits=((s_ve, s_ve.n),), inc=s_sto, inc_k=16)
        w_part(2, x1_t, None)
        if DEBUG == "attnb":
            P.on("gpsimd", lambda e: ld(e, dbg["x2t"].ap(), x1_t[:]),
                 waits=((s_ve, s_ve.n),), inc=s_sto, inc_k=16)
        conv_pass(1, v_t, x1_t, D)
        if DEBUG == "attnb":
            P.on("gpsimd", lambda e: ld(e, dbg["y2"].ap(), v_t[:]),
                 waits=((s_ve, s_ve.n),), inc=s_sto, inc_k=16)

        P.on("sync", lambda e: ld(e, bout_row[:], _ap(b_out, 0, [[1024, 1], [1, 1024]])),
             waits=((s_ve, s_ve.n), (s_te, s_te.n)), inc=s_dma, inc_k=16)
        bo_ld = s_dma.n
        P.on("vector", lambda e: e.memset(ones512b[:], 1.0),
             waits=((s_te, s_te.n), (s_dma, s_dma.n)), inc=s_ve)
        o5_v = s_ve.n
        wostg = A.at("wostg", 16384, [128, 4 * 1024], F32)
        for h in range(2):
            P.on("sync", lambda e, h=h: ld(
                e, wostg[:], _ap(W_out, (h * 4 * 128) * D,
                                 [[D, 128], [128 * D, 4], [1, 1024]])),
                waits=((s_ac, s_ac.n), (s_ve, s_ve.n)), inc=wstsm, inc_k=16)
            P.on("vector", lambda e, h=h: e.tensor_copy(
                wout_sb[:, h * 4096:(h + 1) * 4096], wostg[:]),
                waits=((wstsm, wstsm.n), (s_te, s_te.n)), inc=s_ve)
        wout_v = s_ve.n
        y2ev = [0, 0]
        bank_fr = [0, 0, 0, 0]
        wadd = [0]
        for ts in range(NQ):
            par = ts % 2
            for jj in range(4):
                j = ts * 4 + jj
                P.on("tensor", lambda e, j=j, jj=jj: [e.transpose(
                    pb[:, ct * 128:(ct + 1) * 128],
                    _ap(v_t, (ct * 128) * 33 + j + 1, [[1024 * 33, 128], [33, 128]]),
                    ident_sb[:]) for ct in range(8)][-1],
                    waits=((s_ve, s_ve.n), (s_ve, y2ev[par]) if y2ev[par] else (s_ve, 0)),
                    inc=None)
                tfence()
                t8 = s_te.n
                P.on("vector", lambda e, par=par, jj=jj: e.tensor_copy(
                    _ap(y2c, par * 4096 + jj * 128, [[2 * 4096, 128], [512, 8], [1, 128]]),
                    pb[:]), waits=((s_te, t8),), inc=s_ve)
            y2ev[par] = s_ve.n
            y2rdy = s_ve.n
            for ncb in range(8):
                bank = ncb % 4
                for dk in range(ND):
                    w = [(s_ve, y2rdy), (s_ve, wout_v)]
                    if dk == 0 and bank_fr[bank] > 0:
                        w.append((s_ac, bank_fr[bank]))
                    if dk == 0:
                        w.append((s_dma, bo_ld))
                        w.append((s_ve, o5_v))
                    P.on("tensor", lambda e, ncb=ncb, dk=dk, bank=bank, par=par:
                         e.matmul(
                             s4[:, bank * 512:(bank + 1) * 512],
                             wout_sb[:, dk * 1024 + ncb * 128:
                                     dk * 1024 + ncb * 128 + 128],
                             y2c[:, par * 4096 + dk * 512: par * 4096 + (dk + 1) * 512],
                             start=(dk == 0), stop=False),
                         waits=tuple(w) if dk == 0 else (), inc=None)
                P.on("tensor", lambda e, ncb=ncb, bank=bank: e.matmul(
                    s4[:, bank * 512:(bank + 1) * 512],
                    bout_row[0:1, ncb * 128:(ncb + 1) * 128],
                    ones512b[0:1, :], start=False, stop=True), inc=s_te)
                pfence()
                P.on("scalar", lambda e, ncb=ncb, bank=bank, par=par: e.activation(
                    woc[:, par * 4096 + ncb * 512: par * 4096 + (ncb + 1) * 512],
                    s4[:, bank * 512:(bank + 1) * 512], AF.Identity),
                    waits=(s_te,), inc=s_ac)
                bank_fr[bank] = s_ac.n
            evd = s_ac.n
            for tsub in range(4):
                sp = tsub % 2
                ti = ts * 4 + tsub
                P.on("sync", lambda e, ti=ti, sp=sp: ld(
                    e, z3rd[:, sp * 1024:(sp + 1) * 1024],
                    z3.ap()[ti * 128:(ti + 1) * 128, :]),
                    waits=((s_ve, wadd[0]), (s_z3h[0], s_z3h[0].n),
                           (s_z3h[1], s_z3h[1].n)), inc=z3sm[sp], inc_k=16)
                zld = (z3sm[sp], z3sm[sp].n)
                P.on("tensor", lambda e, tsub=tsub, par=par: [e.transpose(
                    _ap(o2, k * 128, [[1024, 128], [1, 128]]),
                    woc[:, par * 4096 + k * 512 + tsub * 128:
                        par * 4096 + k * 512 + tsub * 128 + 128],
                    identf_sb[:]) for k in range(8)][-1],
                    waits=((s_ac, evd), (s_ve, wadd[0])), inc=None)
                tfence()
                trd = s_te.n
                P.on("vector", lambda e, sp=sp: e.tensor_add(
                    x4st[:, sp * 1024:(sp + 1) * 1024], o2[:],
                    z3rd[:, sp * 1024:(sp + 1) * 1024]),
                    waits=((s_te, trd), zld, (s_r3h[sp], s_r3h[sp].n)), inc=s_ve)
                wadd[0] = s_ve.n
                P.on("gpsimd", lambda e, ti=ti, sp=sp: ld(
                    e, dst.ap()[ti * 128:(ti + 1) * 128, :],
                    x4st[:, sp * 1024:(sp + 1) * 1024]),
                    waits=((s_ve, wadd[0]),), inc=s_r3h[sp], inc_k=16)
        if final:
            P.on("sync", lambda e: None,
                 waits=((s_r3h[0], s_r3h[0].n), (s_r3h[1], s_r3h[1].n)))

    xn4c = dt_("xn4c", [ND, 128, T], F32)
    if (DEBUG in ("sc", "att1") or DEBUG is True) and "attnb" in PHASES:
        P.on("gpsimd", lambda e: ld(e, dbg["r2"].ap(), r2.ap()),
             waits=((s_dsth[0], s_dsth[0].n), (s_dsth[1], s_dsth[1].n)),
             inc=s_scd, inc_k=16)
    if "attnb" in PHASES:
        emit_attnb(r2, r3 if "ffn2" in PHASES else out_t,
                   final=("ffn2" not in PHASES),
                   src_w=((s_dsth[0], s_dsth[0].n), (s_dsth[1], s_dsth[1].n)))
    if (DEBUG in ("sc", "att1") or DEBUG is True) and "ffn2" in PHASES:
        P.on("gpsimd", lambda e: ld(e, dbg["r3"].ap(), r3.ap()),
             waits=((s_r3h[0], s_r3h[0].n), (s_r3h[1], s_r3h[1].n)),
             inc=s_scd, inc_k=16)
    if "ffn2" in PHASES:
        if "attnb" in PHASES:
            emit_wfft()
        emit_ffn(r3, out_t, xn4c, final=True,
                 src_w=((s_r3h[0], s_r3h[0].n), (s_r3h[1], s_r3h[1].n)))

    if PHASES == ("attn",):
        for ti in range(NT):
            par = ti % 2
            P.on("sync", lambda e, ti=ti, par=par: ld(
                e, xtile[:, par * 1024:(par + 1) * 1024],
                r1.ap()[ti * 128:(ti + 1) * 128, :]),
                waits=((s_r1h[0], s_r1h[0].n), (s_r1h[1], s_r1h[1].n),
                       (s_qwh[par], s_qwh[par].n)),
                inc=lnx[par], inc_k=16)
            P.on("gpsimd", lambda e, ti=ti, par=par: ld(
                e, out_t.ap()[ti * 128:(ti + 1) * 128, :],
                xtile[:, par * 1024:(par + 1) * 1024]),
                waits=(lnx[par],), inc=s_qwh[par], inc_k=16)
        P.on("sync", lambda e: None,
             waits=((s_qwh[0], s_qwh[0].n), (s_qwh[1], s_qwh[1].n)))

    P.emit()
    nc._stack.close()
    return nc


def emit_pv(P, kv, p_, s4, o2, vext, p_sb, s_exp, s_pv, lag=0):
    pj = kv % 3
    need = s_exp.n - lag
    for hh in range(2):
        vrng = _ap(vext, kv * 1040 + (p_ * 2 + hh) * 65,
                   [[NKV * 1040, 128], [1, 65]])
        prng = p_sb[:, pj * 1024 + hh * 512: pj * 1024 + hh * 512 + 512]
        P.on("tensor", lambda e, vrng=vrng, prng=prng, hh=hh, kv=kv: e.matmul(
            _ap(o2, hh * 512, [[1024, 65], [1, 512]]), vrng, prng,
            start=(kv == 0), stop=(kv == NKV - 1)),
            waits=((s_exp, need),) if hh == 0 else (),
            inc=(s_pv if hh == 1 else None))


def kernel(**inputs):
    costab, sintab, pmat, ident, identf = host_consts()
    nc = build()
    base = {
        "w_dk": np.ascontiguousarray(inputs["L0_attn_a_w_dk"], np.float32),
        "b_dk": np.ascontiguousarray(inputs["L0_attn_a_b_dk"], np.float32),
        "w_dv": np.ascontiguousarray(inputs["L0_attn_a_w_dv"], np.float32),
        "b_dv": np.ascontiguousarray(inputs["L0_attn_a_b_dv"], np.float32),
        "w_uk": np.ascontiguousarray(inputs["L0_attn_a_w_uk"], np.float32),
        "b_uk": np.ascontiguousarray(inputs["L0_attn_a_b_uk"], np.float32),
        "w_uv": np.ascontiguousarray(inputs["L0_attn_a_w_uv"], np.float32),
        "b_uv": np.ascontiguousarray(inputs["L0_attn_a_b_uv"], np.float32),
        "w_o": np.ascontiguousarray(inputs["L0_attn_a_w_o"], np.float32),
        "b_o": np.ascontiguousarray(inputs["L0_attn_a_b_o"], np.float32),
        "W_in": np.ascontiguousarray(inputs["L0_attn_b_W_in"], np.float32),
        "b_in": np.ascontiguousarray(inputs["L0_attn_b_b_in"], np.float32),
        "k0": np.ascontiguousarray(np.asarray(inputs["L0_attn_b_k0"]).reshape(D, 128), np.float32),
        "k1": np.ascontiguousarray(np.asarray(inputs["L0_attn_b_k1"]).reshape(D, 128), np.float32),
        "gamma": np.ascontiguousarray(inputs["L0_attn_b_gamma"], np.float32),
        "beta": np.ascontiguousarray(inputs["L0_attn_b_beta"], np.float32),
        "W_out": np.ascontiguousarray(inputs["L0_attn_b_W_out"], np.float32),
        "b_out": np.ascontiguousarray(inputs["L0_attn_b_b_out"], np.float32),
        "ffn_w8": np.ascontiguousarray(inputs["L0_ffn_weight_int8"]),
        "ffn_ws": np.ascontiguousarray(np.asarray(inputs["L0_ffn_weight_scale"]).reshape(D), np.float32),
        "ffn_b": np.ascontiguousarray(inputs["L0_ffn_bias"], np.float32),
        "costab": costab, "sintab": sintab, "pmat": pmat,
        "ident": ident, "identf": identf,
    }
    x = np.asarray(inputs["L0_attn_a_x"])
    in_maps = [dict(base, x=np.ascontiguousarray(x[b], np.float32)) for b in range(8)]
    res = run_bass_kernel_spmd(nc, in_maps, core_ids=list(range(8)))
    if DEBUG:
        kernel.dbg_results = res.results
    out = np.stack([np.asarray(res.results[b]["out"]) for b in range(8)])
    return out.astype(np.float32)


# version 12
# speedup vs baseline: 1.1920x; 1.1920x over previous
import numpy as np
import ml_dtypes

import concourse.bass as bass
import concourse.mybir as mybir
from concourse.bass_utils import run_bass_kernel_spmd

F32 = mybir.dt.float32
BF16 = mybir.dt.bfloat16
I8 = mybir.dt.int8
U8 = mybir.dt.uint8
AF = mybir.ActivationFunctionType
ALU = mybir.AluOpType
AX = mybir.AxisListType.X
BF = ml_dtypes.bfloat16

T = 4096
D = 1024
NT = T // 128
ND = D // 128
NQ = T // 512
NKV = T // 128
EPS = 1e-5
MAGIC = float(np.float32(3 * 2 ** 22))
KB = 1024

DEBUG = False
CONV_SERIAL = False
CKV_NBANK = 4
PHASES = ("attn", "ffn1", "attnb", "ffn2")


class Sem:
    def __init__(self, h):
        self.h = h
        self.n = 0


class Prog:
    ENGS = ("tensor", "vector", "scalar", "gpsimd", "sync")

    def __init__(self, nc):
        self.nc = nc
        self.q = {e: [] for e in self.ENGS}
        self._nsem = 0

    def sem(self):
        self._nsem += 1
        return Sem(self.nc._stack.enter_context(
            self.nc.semaphore(name=f"sem{self._nsem}")))

    def on(self, eng, fn, waits=(), inc=None, inc_k=1):
        ws = {}
        for w in waits:
            if isinstance(w, Sem):
                s, v = w, w.n
            else:
                s, v = w
            assert v <= s.n, f"wait on future value {v} > {s.n}"
            if v > 0:
                ws[s.h] = max(ws.get(s.h, 0), v)
        if inc is not None:
            inc.n += inc_k
        self.q[eng].append((list(ws.items()), fn,
                            None if inc is None else (inc.h, inc_k)))

    def emit(self):
        with self.nc.Block() as blk:
            for eng in self.ENGS:
                ops = self.q[eng]
                if not ops:
                    continue

                def body(e, ops=ops):
                    for ws, fn, inc in ops:
                        for (h, v) in ws:
                            e.wait_ge(h, v)
                        inst = fn(e) if fn is not None else None
                        if inc is not None:
                            if inst is None:
                                inst = e.nop()
                            inst.then_inc(inc[0], inc[1])
                getattr(blk, eng)(body)


class Arena:
    def __init__(self, nc, total_bytes):
        self.nc = nc
        self.total = total_bytes
        self.back = nc._stack.enter_context(
            nc.sbuf_tensor("arena", [128, total_bytes], U8))
        self.base = nc.lookup_mloc(self.back).addr

    def at(self, name, addr, shape, dtype):
        per_part = int(np.prod(shape[1:])) * mybir.dt.size(dtype)
        assert addr % 32 == 0, (name, addr)
        assert addr + per_part <= self.total, (name, addr, per_part)
        return self.nc.alloc_sbuf_tensor_at(
            name, shape, dtype, offset=self.base + addr)


def _ap(t, offset, dims):
    return bass.AP(t, offset, [list(d) for d in dims])


def host_consts():
    DH = 64
    inv_freq = 1.0 / (10000.0 ** (np.arange(0, DH, 2, dtype=np.float64) / DH))
    freqs = np.arange(T, dtype=np.float64)[:, None] * inv_freq[None, :]
    emb = np.concatenate([freqs, freqs], axis=-1)
    costab = np.tile(np.cos(emb).astype(np.float32).T, (2, 1))
    sintab = np.tile(np.sin(emb).astype(np.float32).T, (2, 1))
    PT = np.zeros((128, 128), np.float32)
    for h in range(2):
        for dp in range(32):
            PT[h * 64 + dp, h * 64 + dp + 32] = -1.0
        for dp in range(32, 64):
            PT[h * 64 + dp, h * 64 + dp - 32] = 1.0
    pmat = np.ascontiguousarray(PT.T).astype(BF)
    ident = np.eye(128, dtype=BF)
    identf = np.eye(128, dtype=np.float32)
    return costab, sintab, pmat, ident, identf


def build():
    from contextlib import ExitStack
    nc = bass.Bass("TRN2", target_bir_lowering=False, debug=False)
    nc._stack = ExitStack()
    dt_ = nc.dram_tensor

    x_in = dt_("x", [T, D], F32, kind="ExternalInput")
    w_dk = dt_("w_dk", [D, 128], F32, kind="ExternalInput")
    b_dk = dt_("b_dk", [128], F32, kind="ExternalInput")
    w_dv = dt_("w_dv", [D, 128], F32, kind="ExternalInput")
    b_dv = dt_("b_dv", [128], F32, kind="ExternalInput")
    w_uk = dt_("w_uk", [128, D], F32, kind="ExternalInput")
    b_uk = dt_("b_uk", [D], F32, kind="ExternalInput")
    w_uv = dt_("w_uv", [128, D], F32, kind="ExternalInput")
    b_uv = dt_("b_uv", [D], F32, kind="ExternalInput")
    w_o = dt_("w_o", [D, D], F32, kind="ExternalInput")
    b_o = dt_("b_o", [D], F32, kind="ExternalInput")
    W_in = dt_("W_in", [D, 3 * D], F32, kind="ExternalInput")
    b_in = dt_("b_in", [3 * D], F32, kind="ExternalInput")
    k0 = dt_("k0", [D, 128], F32, kind="ExternalInput")
    k1 = dt_("k1", [D, 128], F32, kind="ExternalInput")
    gamma = dt_("gamma", [D], F32, kind="ExternalInput")
    beta = dt_("beta", [D], F32, kind="ExternalInput")
    W_out = dt_("W_out", [D, D], F32, kind="ExternalInput")
    b_out = dt_("b_out", [D], F32, kind="ExternalInput")
    ffn_w8 = dt_("ffn_w8", [D, D], I8, kind="ExternalInput")
    ffn_ws = dt_("ffn_ws", [D], F32, kind="ExternalInput")
    ffn_b = dt_("ffn_b", [D], F32, kind="ExternalInput")
    costab_in = dt_("costab", [128, T], F32, kind="ExternalInput")
    sintab_in = dt_("sintab", [128, T], F32, kind="ExternalInput")
    pmat_in = dt_("pmat", [128, 128], BF16, kind="ExternalInput")
    ident_in = dt_("ident", [128, 128], BF16, kind="ExternalInput")
    identf_in = dt_("identf", [128, 128], F32, kind="ExternalInput")
    out_t = dt_("out", [T, D], F32, kind="ExternalOutput")

    r1 = dt_("r1", [T, D], F32)
    r2 = dt_("r2", [T, D], F32)
    r3 = dt_("r3", [T, D], F32)
    z3 = dt_("z3", [T, D], F32)
    xnbc = dt_("xnbc", [NT, 128, ND * 128], BF16)
    qrope = dt_("qrope", [ND, 128, T], BF16)
    kfwd = dt_("kfwd", [2, D, 384], BF16)
    toep = dt_("toep", [2, D, 128, 256], BF16)
    wo_bf_d = dt_("wo_bf", [ND, 128, D], BF16)

    dbg = {}
    if DEBUG == "attnb":
        dbg["xnbc"] = dt_("dbg_xnbc", [NT, 128, ND * 128], BF16, kind="ExternalOutput")
        dbg["z3"] = dt_("dbg_z3", [T, D], F32, kind="ExternalOutput")
        dbg["vt"] = dt_("dbg_vt", [128, 1024 * 33], BF16, kind="ExternalOutput")
        dbg["x1t"] = dt_("dbg_x1t", [128, 1024 * 33], BF16, kind="ExternalOutput")
        dbg["y1"] = dt_("dbg_y1", [128, 1024 * 33], BF16, kind="ExternalOutput")
        dbg["y2"] = dt_("dbg_y2", [128, 1024 * 33], BF16, kind="ExternalOutput")
        dbg["x2t"] = dt_("dbg_x2t", [128, 1024 * 33], BF16, kind="ExternalOutput")
    if DEBUG in ("sc", "att1") or DEBUG is True:
        dbg["sc"] = dt_("dbg_sc", [1, 8], F32, kind="ExternalOutput")
        dbg["r1"] = dt_("dbg_r1", [T, D], F32, kind="ExternalOutput")
        dbg["r2"] = dt_("dbg_r2", [T, D], F32, kind="ExternalOutput")
        dbg["r3"] = dt_("dbg_r3", [T, D], F32, kind="ExternalOutput")
    if DEBUG is True or DEBUG == "att1":
        dbg["xn1c"] = dt_("dbg_xn1c", [128, ND * T], BF16, kind="ExternalOutput")
        dbg["ck"] = dt_("dbg_ck", [128, T], BF16, kind="ExternalOutput")
        dbg["krope"] = dt_("dbg_krope", [128, ND * T], BF16, kind="ExternalOutput")
        dbg["vext"] = dt_("dbg_vext", [128, NKV * 1040], BF16, kind="ExternalOutput")
        dbg["buv"] = dt_("dbg_buv", [1, D], F32, kind="ExternalOutput")
    if DEBUG is True:
        dbg["qr"] = dt_("dbg_qr", [128, ND * T], BF16, kind="ExternalOutput")

    P = Prog(nc)
    A = Arena(nc, 192 * KB)

    ident_sb = A.at("identsb", 0, [128, 128], BF16)
    pmat_sb = A.at("pmatsb", 256, [128, 128], BF16)
    onesf = A.at("onesf", 512, [1, 128], F32)
    identf_sb = A.at("identfsb", 1024, [128, 128], F32)
    bdk_sb = A.at("bdksb", 1536, [128, 1], F32)
    bdv_sb = A.at("bdvsb", 1568, [128, 1], F32)
    buk_sb = A.at("buksb", 1600, [128, 8], F32)
    wsc_sb = A.at("wscsb", 1632, [128, 8], F32)
    bff_sb = A.at("bffsb", 1664, [128, 8], F32)
    ssw_sb = A.at("sswsb", 1696, [128, 8], F32)
    gam_sb = A.at("gamsb", 1728, [128, 8], F32)
    bout_sb = A.at("boutsb", 1760, [128, 8], F32)
    beta_sb = A.at("betasb", 1792, [128, 8], F32)
    mgc_sb = A.at("mgcsb", 1824, [128, 1], F32)
    invs_sb = A.at("invssb", 1856, [128, 1], F32)
    sml = A.at("sml", 1888, [128, 24], F32)
    wdk_sb = A.at("wdksb", 2048, [128, ND * 128], BF16)
    wdv_sb = A.at("wdvsb", 4096, [128, ND * 128], BF16)
    wuk_sb = A.at("wuksb", 6144, [128, D], BF16)
    wuv_sb = A.at("wuvsb", 8192, [128, D], BF16)
    borow = A.at("borow", 10240, [1, D], F32)

    maxs_sb = A.at("maxssb", 14336, [128, 32], F32)
    wstg = A.at("wstg", 14464, [128, 256], F32)
    wcst = A.at("wcst", 15488, [128, 256], BF16)

    s4 = nc._stack.enter_context(nc.psum_tensor("ps_s4", [128, 2048], F32))
    o2 = nc._stack.enter_context(nc.psum_tensor("ps_o2", [128, 1024], F32))
    aux = nc._stack.enter_context(nc.psum_tensor("ps_aux", [128, 512], F32))
    pb = nc._stack.enter_context(nc.psum_tensor("ps_pb", [128, 1024], BF16))

    TB = 16 * KB
    tabc = A.at("tabc", TB, [128, 2 * 512], F32)
    tabs_ = A.at("tabs", TB + 4096, [128, 2 * 512], F32)
    ck_sb = A.at("cksb", 24576, [128, T], BF16)
    cv_sb = A.at("cvsb", 32768, [128, T], BF16)
    xn1c = A.at("xn1c", 40960, [128, ND * T], BF16)
    krope = A.at("krope", 40960, [128, ND * T], BF16)
    vext = A.at("vext", 106496, [128, NKV * 1040], BF16)
    tmp1 = A.at("tmp1", 173056, [128, 512], F32)
    tmp2 = A.at("tmp2", 175104, [128, 512], F32)
    ktmp = A.at("ktmp", 177152, [128, 512], BF16)
    lnstg = A.at("lnstg", 178176, [128, 1024], BF16)
    xtile = A.at("xtile", 180224, [128, 2 * 1024], F32)
    x2stg = A.at("x2stg", 188416, [128, 2 * 1024], F32)
    bdk_row = A.at("bdk_row", 188416, [1, 128], F32)
    bdv_row = A.at("bdv_row", 188928, [1, 128], F32)
    buk_row = A.at("buk_row", 189440, [1, D], F32)
    ones512 = A.at("ones512", 193536, [1, 512], F32)
    sqs = A.at("sqs", 24576, [128, 2 * 1024], F32)
    kstg = A.at("kstg", 33792, [128, 128], F32)
    kfb = A.at("kfb", 34816, [128, 384], BF16)
    buvrow = A.at("buvrow", 173056, [1, D], F32)
    qpair = A.at("qpair", TB, [128, 2 * 512], BF16)
    p_sb = A.at("p_sb", TB + 2048, [128, 3 * 1024], BF16)
    oc_sb = A.at("oc_sb", TB + 8192, [128, ND * 512], BF16)
    rbc_sb = A.at("rbc_sb", TB + 16384, [128, 512], F32)
    wob = A.at("wob", TB + 18432, [128, 2 * 1024], BF16)
    rr_sb = A.at("rr_sb", 175104, [1, 1024], F32)

    def pfence(out=None, waits=(), inc=None):
        P.on("tensor", lambda e, out=out: e.matmul(
            out if out is not None else aux[0:1, 0:1],
            ident_sb[:, 0:1], ident_sb[:, 0:1],
            start=True, stop=True), waits=waits, inc=(inc if inc else s_te))

    def tfence(waits=()):
        pfence(waits=waits)

    s_dma = P.sem()
    lnx = [P.sem(), P.sem()]
    tabsm = [P.sem(), P.sem()]
    qpsm = [P.sem(), P.sem()]
    wobsm = [P.sem(), P.sem()]
    fxnsm = [P.sem(), P.sem()]
    fx2sm = [P.sem(), P.sem()]
    z3sm = [P.sem(), P.sem()]
    bxbsm = [P.sem(), P.sem()]
    tpsm = [P.sem() for _ in range(8)]
    wstsm = P.sem()
    kstsm = P.sem()
    s_qwh = [P.sem(), P.sem()]
    s_kf = P.sem()
    s_tp = P.sem()
    s_wob = P.sem()
    s_r1h = [P.sem(), P.sem()]
    s_dsth = [P.sem(), P.sem()]
    s_z3h = [P.sem(), P.sem()]
    s_xbst = P.sem()
    s_xch = [P.sem(), P.sem()]
    s_r3h = [P.sem(), P.sem()]
    s_ccio = P.sem()
    s_scd = P.sem()
    s_sto = P.sem()
    s_ve = P.sem()
    s_ac = P.sem()
    s_te = P.sem()
    s_qk = P.sem()
    s_exp = P.sem()
    s_pv = P.sem()
    s_ev = P.sem()
    s_wo = P.sem()

    def ld(e, dst, src, **kw):
        return e.dma_start(out=dst, in_=src, **kw)

    P.on("sync", lambda e: ld(e, ident_sb[:], ident_in.ap()), inc=s_dma, inc_k=16)
    P.on("sync", lambda e: ld(e, pmat_sb[:], pmat_in.ap()), inc=s_dma, inc_k=16)
    P.on("sync", lambda e: ld(e, identf_sb[:], identf_in.ap()), inc=s_dma, inc_k=16)
    for t, src in ((buk_sb, b_uk), (wsc_sb, ffn_ws), (bff_sb, ffn_b),
                   (gam_sb, gamma), (bout_sb, b_out), (beta_sb, beta)):
        P.on("sync", lambda e, t=t, src=src: ld(
            e, t[:], _ap(src, 0, [[1, 128], [128, 8]]),
            allow_slow_non_contiguous=True), inc=s_dma, inc_k=16)
    P.on("sync", lambda e: ld(e, bdk_sb[:], _ap(b_dk, 0, [[1, 128], [1, 1]]),
                              allow_slow_non_contiguous=True), inc=s_dma, inc_k=16)
    P.on("sync", lambda e: ld(e, bdv_sb[:], _ap(b_dv, 0, [[1, 128], [1, 1]]),
                              allow_slow_non_contiguous=True), inc=s_dma, inc_k=16)
    P.on("sync", lambda e: ld(e, borow[:], _ap(b_o, 0, [[0, 1], [1, 1024]])),
         inc=s_dma, inc_k=16)
    P.on("vector", lambda e: e.memset(onesf[:], 1.0), inc=s_ve)
    P.on("vector", lambda e: e.memset(ones512[:], 1.0), inc=s_ve)
    P.on("vector", lambda e: e.memset(mgc_sb[:], MAGIC), inc=s_ve)
    P.on("sync", lambda e: ld(e, bdk_row[:], _ap(b_dk, 0, [[128, 1], [1, 128]])),
         inc=s_dma, inc_k=16)
    P.on("sync", lambda e: ld(e, bdv_row[:], _ap(b_dv, 0, [[128, 1], [1, 128]])),
         inc=s_dma, inc_k=16)
    P.on("sync", lambda e: ld(e, buk_row[:], _ap(b_uk, 0, [[1024, 1], [1, 1024]])),
         inc=s_dma, inc_k=16)
    prep_t = s_dma.n

    wdkf = A.at("wdkf", 106496, [128, 1024], F32)
    wdvf = A.at("wdvf", 110592, [128, 1024], F32)
    wukf = A.at("wukf", 114688, [128, 1024], F32)
    wuvf = A.at("wuvf", 118784, [128, 1024], F32)
    wof = A.at("wof", 122880, [128, 8 * 1024], F32)
    wob16 = A.at("wob16", 155648, [128, 8 * 1024], BF16)
    kfb2 = A.at("kfb2", 122880, [128, 2 * 384], BF16)
    kstg2 = A.at("kstg2", 16384, [128, 2 * 1024], F32)
    bigld = P.sem()
    P.on("sync", lambda e: ld(
        e, wdkf[:], _ap(w_dk, 0, [[128, 128], [128 * 128, 8], [1, 128]])),
        inc=bigld, inc_k=16)
    P.on("sync", lambda e: ld(
        e, wdvf[:], _ap(w_dv, 0, [[128, 128], [128 * 128, 8], [1, 128]])),
        inc=bigld, inc_k=16)
    P.on("sync", lambda e: ld(
        e, wukf[:], _ap(w_uk, 0, [[1024, 128], [1, 1024]])), inc=bigld, inc_k=16)
    P.on("sync", lambda e: ld(
        e, wuvf[:], _ap(w_uv, 0, [[1024, 128], [1, 1024]])), inc=bigld, inc_k=16)
    P.on("sync", lambda e: ld(
        e, wof[:], _ap(w_o, 0, [[1024, 128], [128 * 1024, 8], [1, 1024]])),
        inc=bigld, inc_k=16)
    P.on("sync", lambda e: ld(
        e, kstg2[:, 0:1024], _ap(k0, 0, [[128, 128], [128 * 128, 8], [1, 128]])),
        inc=bigld, inc_k=16)
    P.on("sync", lambda e: ld(
        e, kstg2[:, 1024:2048], _ap(k1, 0, [[128, 128], [128 * 128, 8], [1, 128]])),
        inc=bigld, inc_k=16)
    bld_all = bigld.n
    for dst_sb, stg in ((wdk_sb, wdkf), (wdv_sb, wdvf),
                        (wuk_sb, wukf), (wuv_sb, wuvf)):
        P.on("vector", lambda e, dst_sb=dst_sb, stg=stg: e.tensor_copy(
            dst_sb[:], stg[:]), waits=((bigld, bld_all),), inc=s_ve)
    wcast_gp = 0

    P.on("vector", lambda e: e.tensor_copy(wob16[:], wof[:]),
         waits=((bigld, bld_all),), inc=s_ve)
    wo_cv = s_ve.n
    P.on("gpsimd", lambda e: ld(
        e, _ap(wo_bf_d, 0, [[1024, 128], [128 * 1024, 8], [1, 1024]]), wob16[:]),
        waits=((s_ve, wo_cv),), inc=s_wob, inc_k=16)

    kf_st = [0, 0]
    for it in range(16):
        oi, c8 = it // 8, it % 8
        sl = it % 2
        P.on("vector", lambda e, sl=sl: e.memset(
            kfb2[:, sl * 384:(sl + 1) * 384], 0.0),
            waits=((s_kf, kf_st[sl]),) if kf_st[sl] else (), inc=s_ve)
        P.on("vector", lambda e, oi=oi, c8=c8, sl=sl: e.tensor_copy(
            kfb2[:, sl * 384 + 127: sl * 384 + 255],
            kstg2[:, oi * 1024 + c8 * 128: oi * 1024 + (c8 + 1) * 128]),
            waits=((bigld, bld_all), (s_ve, s_ve.n)), inc=s_ve)
        kf_c = s_ve.n
        P.on("gpsimd", lambda e, oi=oi, c8=c8, sl=sl: ld(
            e, _ap(kfwd, (oi * D + c8 * 128) * 384, [[384, 128], [1, 384]]),
            kfb2[:, sl * 384:(sl + 1) * 384]), waits=((s_ve, kf_c),),
            inc=s_kf, inc_k=16)
        kf_st[sl] = s_kf.n
    kf_done = s_kf.n

    toep_dmas = [(oi, p_) for oi in range(2) for p_ in range(128)]

    def emit_toep(n):
        for _ in range(n):
            if not toep_dmas:
                return
            oi, p_ = toep_dmas.pop(0)
            P.on("gpsimd", lambda e, oi=oi, p_=p_: ld(
                e, _ap(toep, (oi * D) * 128 * 256 + p_ * 256,
                       [[128 * 256, 1024], [1, 256]]),
                _ap(kfwd, (oi * D) * 384 + 127 - p_, [[384, 1024], [1, 256]])),
                waits=((s_kf, kf_done),), inc=s_tp, inc_k=16)

    ln_cons = [0, 0]
    sq_red = [0, 0]
    ln_tp = [0]
    for i in range(NT):
        par = i % 2
        sm = i % 2
        w_war = ((s_ve, ln_cons[par]),) if i >= 2 else ()
        P.on("sync", lambda e, i=i, par=par: ld(
            e, xtile[:, par * 1024:(par + 1) * 1024],
            x_in.ap()[i * 128:(i + 1) * 128, :]),
            waits=w_war, inc=lnx[par], inc_k=16)
        xld = lnx[par].n
        xt = xtile[:, par * 1024:(par + 1) * 1024]
        P.on("vector", lambda e, xt=xt, sm=sm: e.reduce_sum(
            sml[:, sm:sm + 1], xt, axis=AX),
            waits=((lnx[par], xld), (s_dma, prep_t)), inc=s_ve)
        sq_w = [(lnx[par], xld), (s_ve, sq_red[par])]
        P.on("scalar", lambda e, xt=xt, par=par: e.activation(
            sqs[:, par * 1024:(par + 1) * 1024], xt, AF.Square),
            waits=tuple(sq_w) + ((s_dma, prep_t),), inc=s_ac)
        P.on("vector", lambda e, sm=sm, par=par: e.reduce_sum(
            sml[:, 2 + sm:3 + sm], sqs[:, par * 1024:(par + 1) * 1024], axis=AX),
            waits=(s_ac,), inc=s_ve)
        sq_red[par] = s_ve.n
        P.on("vector", lambda e, sm=sm: e.tensor_scalar(
            sml[:, 4 + sm:5 + sm], sml[:, sm:sm + 1], -1.0 / D, None, ALU.mult),
            waits=(s_ve,), inc=s_ve)
        P.on("vector", lambda e, sm=sm: e.tensor_mul(
            sml[:, 6 + sm:7 + sm], sml[:, sm:sm + 1], sml[:, sm:sm + 1]),
            waits=(s_ve,), inc=s_ve)
        P.on("vector", lambda e, sm=sm: e.tensor_scalar(
            sml[:, 8 + sm:9 + sm], sml[:, 2 + sm:3 + sm], 1.0 / D, EPS,
            ALU.mult, ALU.add), waits=(s_ve,), inc=s_ve)
        P.on("vector", lambda e, sm=sm: e.scalar_tensor_tensor(
            sml[:, 10 + sm:11 + sm], sml[:, 6 + sm:7 + sm], -1.0 / (D * D),
            sml[:, 8 + sm:9 + sm], ALU.mult, ALU.add), waits=(s_ve,), inc=s_ve)
        P.on("scalar", lambda e, sm=sm: e.activation(
            sml[:, 12 + sm:13 + sm], sml[:, 10 + sm:11 + sm], AF.Ln),
            waits=(s_ve,), inc=s_ac)
        P.on("scalar", lambda e, sm=sm: e.activation(
            sml[:, 14 + sm:15 + sm], sml[:, 12 + sm:13 + sm], AF.Exp, scale=-0.5),
            waits=(s_ac,), inc=s_ac)
        P.on("vector", lambda e, xt=xt, sm=sm: e.tensor_scalar(
            lnstg[:], xt, sml[:, 4 + sm:5 + sm], sml[:, 14 + sm:15 + sm],
            ALU.add, ALU.mult),
            waits=((s_ac, s_ac.n), (s_te, ln_tp[0])), inc=s_ve)
        nrm = s_ve.n
        ln_cons[par] = s_ve.n
        P.on("tensor", lambda e: [e.transpose(
            pb[:, k * 128:(k + 1) * 128], lnstg[:, k * 128:(k + 1) * 128],
            ident_sb[:]) for k in range(8)][-1],
            waits=((s_ve, nrm), (s_ve, s_ve.n), (s_dma, prep_t)), inc=None)
        tfence()
        ln_tp[0] = s_te.n
        P.on("vector", lambda e, i=i: e.tensor_copy(
            _ap(xn1c, i * 128, [[ND * T, 128], [T, 8], [1, 128]]), pb[:]),
            waits=(s_te,), inc=s_ve)
    ln1_v = s_ve.n
    if DEBUG is True or DEBUG == "att1":
        P.on("gpsimd", lambda e: ld(e, dbg["xn1c"].ap(), xn1c[:]),
             waits=((s_ve, ln1_v),), inc=s_sto, inc_k=16)

    for dk in range(ND):
        for qt in range(NQ):
            it = dk * NQ + qt
            par = it % 2
            P.on("sync", lambda e, qt=qt, par=par: ld(
                e, tabc[:, par * 512:(par + 1) * 512],
                costab_in.ap()[:, qt * 512:(qt + 1) * 512]),
                waits=((s_ve, s_ve.n),), inc=tabsm[par], inc_k=16)
            P.on("sync", lambda e, qt=qt, par=par: ld(
                e, tabs_[:, par * 512:(par + 1) * 512],
                sintab_in.ap()[:, qt * 512:(qt + 1) * 512]), inc=tabsm[par], inc_k=16)
            dma_t = (tabsm[par], tabsm[par].n)
            xsl = xn1c[:, dk * T + qt * 512: dk * T + qt * 512 + 512]
            P.on("tensor", lambda e, xsl=xsl, par=par: e.matmul(
                o2[:, par * 512:(par + 1) * 512], pmat_sb[:], xsl,
                start=True, stop=True),
                waits=((s_ve, s_ve.n),), inc=s_te)
            pfence()
            pm_t = s_te.n
            P.on("vector", lambda e, xsl=xsl, par=par: e.tensor_mul(
                tmp1[:], xsl, tabc[:, par * 512:(par + 1) * 512]),
                waits=(dma_t, (s_ve, s_ve.n)), inc=s_ve)
            P.on("vector", lambda e, par=par: e.tensor_mul(
                tmp2[:], o2[:, par * 512:(par + 1) * 512],
                tabs_[:, par * 512:(par + 1) * 512]),
                waits=((s_te, pm_t), (s_ve, s_ve.n)), inc=s_ve)
            P.on("vector", lambda e, par=par: e.tensor_add(
                _ap(lnstg, par * 512, [[1024, 128], [1, 512]]), tmp1[:], tmp2[:]),
                waits=((s_ve, s_ve.n), (s_qwh[par], s_qwh[par].n)), inc=s_ve)
            qd = s_ve.n
            P.on("gpsimd", lambda e, dk=dk, qt=qt, par=par: ld(
                e, _ap(qrope, (dk * 128) * T + qt * 512, [[T, 128], [1, 512]]),
                _ap(lnstg, par * 512, [[1024, 128], [1, 512]])),
                waits=((s_ve, qd),), inc=s_qwh[par], inc_k=16)
    if DEBUG is True:
        for dk in range(ND):
            P.on("sync", lambda e, dk=dk: ld(
                e, _ap(lnstg, 0, [[1024, 128], [1, 512]]),
                _ap(qrope, dk * 128 * T, [[T, 128], [1, 512]])),
                waits=((s_sto, s_sto.n),), inc=s_dma, inc_k=16)
            P.on("gpsimd", lambda e, dk=dk: ld(
                e, _ap(dbg["qr"], dk * T, [[ND * T, 128], [1, 512]]),
                _ap(lnstg, 0, [[1024, 128], [1, 512]])),
                waits=(s_dma,), inc=s_sto, inc_k=16)

    bank_ev = [0, 0, 0, 0]
    for wsb, _bsb, dst in ((wdk_sb, bdk_sb, ck_sb), (wdv_sb, bdv_sb, cv_sb)):
        brow = bdk_row if dst is ck_sb else bdv_row
        for ts in range(NQ):
            bank = ts % 4
            for dk in range(ND):
                w = [(s_ve, ln1_v)]
                if dk == 0 and bank_ev[bank] > 0:
                    w.append((s_ac, bank_ev[bank]))
                P.on("tensor", lambda e, wsb=wsb, dk=dk, ts=ts, bank=bank: e.matmul(
                    s4[:, bank * 512:(bank + 1) * 512],
                    wsb[:, dk * 128:(dk + 1) * 128],
                    xn1c[:, dk * T + ts * 512: dk * T + ts * 512 + 512],
                    start=(dk == 0), stop=False),
                    waits=tuple(w), inc=None)
            P.on("tensor", lambda e, brow=brow, bank=bank: e.matmul(
                s4[:, bank * 512:(bank + 1) * 512], brow[0:1, :],
                ones512[0:1, :], start=False, stop=True), inc=s_te)
            pfence()
            P.on("scalar", lambda e, dst=dst, ts=ts, bank=bank: e.activation(
                dst[:, ts * 512:(ts + 1) * 512], s4[:, bank * 512:(bank + 1) * 512],
                AF.Identity),
                waits=(s_te,), inc=s_ac)
            bank_ev[bank] = s_ac.n
    ckv_ac = s_ac.n
    if DEBUG is True or DEBUG == "att1":
        P.on("gpsimd", lambda e: ld(e, dbg["ck"].ap(), ck_sb[:]),
             waits=((s_ac, ckv_ac),), inc=s_sto, inc_k=16)

    bank_cons = [0, 0, 0, 0]
    for dk in range(ND):
        for ts in range(NQ):
            it = dk * NQ + ts
            par = it % 2
            bank = it % 4
            P.on("sync", lambda e, ts=ts, par=par: ld(
                e, tabc[:, par * 512:(par + 1) * 512],
                costab_in.ap()[:, ts * 512:(ts + 1) * 512]),
                waits=((s_ve, s_ve.n),), inc=tabsm[par], inc_k=16)
            P.on("sync", lambda e, ts=ts, par=par: ld(
                e, tabs_[:, par * 512:(par + 1) * 512],
                sintab_in.ap()[:, ts * 512:(ts + 1) * 512]), inc=tabsm[par], inc_k=16)
            dma_t = (tabsm[par], tabsm[par].n)
            w = [(s_ac, ckv_ac)]
            if bank_cons[bank] > 0:
                w.append((s_ac, bank_cons[bank]))
            P.on("tensor", lambda e, dk=dk, ts=ts, bank=bank: e.matmul(
                s4[:, bank * 512:(bank + 1) * 512],
                wuk_sb[:, dk * 128:(dk + 1) * 128],
                ck_sb[:, ts * 512:(ts + 1) * 512], start=True, stop=False),
                waits=tuple(w), inc=None)
            P.on("tensor", lambda e, dk=dk, bank=bank: e.matmul(
                s4[:, bank * 512:(bank + 1) * 512],
                buk_row[0:1, dk * 128:(dk + 1) * 128],
                ones512[0:1, :], start=False, stop=True), inc=s_te)
            pfence()
            kb_t = s_te.n
            P.on("scalar", lambda e, dk=dk, bank=bank: e.activation(
                ktmp[:], s4[:, bank * 512:(bank + 1) * 512], AF.Identity),
                waits=((s_te, kb_t), (s_ve, s_ve.n)), inc=s_ac)
            kev = s_ac.n
            bank_cons[bank] = kev
            P.on("tensor", lambda e, par=par: e.matmul(
                o2[:, par * 512:(par + 1) * 512], pmat_sb[:], ktmp[:],
                start=True, stop=True),
                waits=((s_ac, kev), (s_ve, s_ve.n)), inc=s_te)
            pfence()
            pm_t = s_te.n
            P.on("vector", lambda e, par=par: e.tensor_mul(
                tmp1[:], ktmp[:], tabc[:, par * 512:(par + 1) * 512]),
                waits=((s_ac, kev), dma_t, (s_ve, s_ve.n)), inc=s_ve)
            P.on("vector", lambda e, par=par: e.tensor_mul(
                tmp2[:], o2[:, par * 512:(par + 1) * 512],
                tabs_[:, par * 512:(par + 1) * 512]),
                waits=((s_te, pm_t), (s_ve, s_ve.n)), inc=s_ve)
            P.on("vector", lambda e, dk=dk, ts=ts: e.tensor_add(
                krope[:, dk * T + ts * 512: dk * T + ts * 512 + 512],
                tmp1[:], tmp2[:]), waits=(s_ve,), inc=s_ve)
    krope_v = s_ve.n
    if DEBUG is True or DEBUG == "att1":
        P.on("gpsimd", lambda e: ld(e, dbg["krope"].ap(), krope[:]),
             waits=((s_ve, krope_v),), inc=s_sto, inc_k=16)

    P.on("sync", lambda e: ld(e, buvrow[:], _ap(b_uv, 0, [[0, 1], [1, 1024]])),
         waits=((s_ve, krope_v),), inc=s_dma, inc_k=16)
    buv_t = s_dma.n
    P.on("vector", lambda e: e.memset(
        _ap(vext, 64, [[NKV * 1040, 128], [1040, NKV], [65, 16]]), 1.0),
        waits=((s_ve, krope_v), (s_wob, s_wob.n), (s_kf, kf_done)), inc=s_ve)
    vones = s_ve.n
    prev_vev = [0]
    for tt in range(NT):
        w = [(s_ve, vones), (s_dma, buv_t), (s_ac, ckv_ac)]
        if prev_vev[0] > 0:
            w.append((s_ac, prev_vev[0]))
        for nh in range(2):
            P.on("tensor", lambda e, tt=tt, nh=nh: e.matmul(
                o2[:, nh * 512:(nh + 1) * 512], cv_sb[:, tt * 128:(tt + 1) * 128],
                wuv_sb[:, nh * 512:(nh + 1) * 512], start=True, stop=False),
                waits=tuple(w) if nh == 0 else (), inc=None)
            P.on("tensor", lambda e, nh=nh: e.matmul(
                o2[:, nh * 512:(nh + 1) * 512], onesf[0:1, :],
                buvrow[0:1, nh * 512:(nh + 1) * 512], start=False, stop=True),
                inc=(s_te if nh == 1 else None))
        pfence()
        P.on("scalar", lambda e, tt=tt: e.activation(
            _ap(vext, tt * 1040, [[NKV * 1040, 128], [65, 16], [1, 64]]),
            _ap(o2, 0, [[1024, 128], [64, 16], [1, 64]]),
            AF.Identity), waits=(s_te,), inc=s_ac)
        prev_vev[0] = s_ac.n
    v_ac = s_ac.n
    if DEBUG == "att1":
        P.on("gpsimd", lambda e: ld(e, dbg["buv"].ap(), buvrow[:]),
             waits=((s_ac, v_ac),), inc=s_sto, inc_k=16)
    if DEBUG is True or DEBUG == "att1":
        P.on("gpsimd", lambda e: ld(e, dbg["vext"].ap(), vext[:]),
             waits=((s_ac, v_ac),), inc=s_sto, inc_k=16)

    for q in range(NQ):
        for p_ in range(ND):
            qp = (q * ND + p_) % 2
            P.on("sync", lambda e, q=q, p_=p_, qp=qp: ld(
                e, qpair[:, qp * 512:(qp + 1) * 512],
                _ap(qrope, (p_ * 128) * T + q * 512, [[T, 128], [1, 512]])),
                waits=((s_qk, s_qk.n), (s_qwh[0], s_qwh[0].n),
                       (s_qwh[1], s_qwh[1].n)), inc=qpsm[qp], inc_k=16)
            qld = (qpsm[qp], qpsm[qp].n)
            for kv in range(NKV):
                b = kv % 2
                ws = [(s_exp, max(0, s_exp.n - 1))]
                if kv == 0:
                    ws += [(s_ev, s_ev.n), qld, (s_ac, v_ac),
                           (s_ve, s_ve.n), (s_te, s_te.n)]
                for hh in range(2):
                    for cg in range(2):
                        krng = krope[hh * 64:(hh + 1) * 64,
                                     p_ * T + kv * 128 + cg * 64:
                                     p_ * T + kv * 128 + cg * 64 + 64]
                        qrng = qpair[hh * 64:(hh + 1) * 64,
                                     qp * 512:(qp + 1) * 512]
                        first = (hh == 0 and cg == 0)
                        P.on("tensor", lambda e, b=b, hh=hh, cg=cg, krng=krng,
                             qrng=qrng: e.matmul(
                                 _ap(s4, cg * 64 * 2048 + b * 1024 + hh * 512,
                                     [[2048, 64], [1, 512]]),
                                 krng, qrng, start=True, stop=True,
                                 tile_position=(hh * 64, cg * 64)),
                             waits=tuple(ws) if first else (),
                             inc=None)
                pfence(inc=s_qk)
                P.on("scalar", lambda e, b=b, pj=kv % 3: e.activation(
                    p_sb[:, pj * 1024:(pj + 1) * 1024],
                    s4[:, b * 1024:(b + 1) * 1024], AF.Exp, scale=0.125),
                    waits=((s_qk, s_qk.n), (s_pv, s_pv.n)), inc=s_exp)
                if kv >= 2:
                    emit_pv(P, kv - 2, p_, s4, o2, vext, p_sb, s_exp, s_pv, lag=2)
            emit_pv(P, NKV - 2, p_, s4, o2, vext, p_sb, s_exp, s_pv, lag=1)
            emit_pv(P, NKV - 1, p_, s4, o2, vext, p_sb, s_exp, s_pv, lag=0)
            pfence(inc=s_pv)
            P.on("vector", lambda e: e.reciprocal(
                rr_sb[0:1, 0:512], _ap(o2, 64 * 1024, [[1024, 1], [1, 512]])),
                waits=((s_pv, s_pv.n), (s_ve, s_ve.n), (s_te, s_te.n)), inc=s_ve)
            P.on("vector", lambda e: e.reciprocal(
                rr_sb[0:1, 512:1024], _ap(o2, 64 * 1024 + 512, [[1024, 1], [1, 512]])),
                waits=(s_ve,), inc=s_ve)
            rcp = s_ve.n
            P.on("tensor", lambda e: e.matmul(
                aux[0:64, :], onesf[0:1, 0:64], rr_sb[0:1, 0:512],
                start=True, stop=True, tile_position=(0, 0)),
                waits=((s_ve, rcp),), inc=None)
            P.on("tensor", lambda e: e.matmul(
                aux[64:128, :], onesf[0:1, 0:64], rr_sb[0:1, 512:1024],
                start=True, stop=True, tile_position=(0, 64)), inc=s_te)
            pfence(out=s4[0:1, 0:1])
            rbc_t = s_te.n
            P.on("vector", lambda e: e.tensor_copy(rbc_sb[:], aux[:]),
                 waits=((s_te, rbc_t),), inc=s_ve)
            cp = s_ve.n
            P.on("vector", lambda e, p_=p_: e.tensor_mul(
                oc_sb[0:64, p_ * 512:(p_ + 1) * 512],
                _ap(o2, 0, [[1024, 64], [1, 512]]), rbc_sb[0:64, :]),
                waits=((s_ve, cp), (s_te, s_te.n)), inc=s_ve)
            P.on("vector", lambda e, p_=p_: e.tensor_mul(
                oc_sb[64:128, p_ * 512:(p_ + 1) * 512],
                _ap(o2, 512, [[1024, 64], [1, 512]]), rbc_sb[64:128, :]),
                waits=(s_ve,), inc=s_ev)
            emit_toep(4)
        oc_ready = s_ev.n
        for tp in range(2):
            for dk in range(ND):
                wp = dk % 2
                P.on("sync", lambda e, dk=dk, wp=wp: ld(
                    e, wob[:, wp * 1024:(wp + 1) * 1024],
                    _ap(wo_bf_d, dk * 128 * D, [[D, 128], [1, D]])),
                    waits=((s_wo, max(0, s_wo.n - 1)), (s_wob, s_wob.n)),
                    inc=wobsm[wp], inc_k=16)
                wo_ld = (wobsm[wp], wobsm[wp].n)
                for ts2 in range(2):
                    tsub = tp * 2 + ts2
                    for nh in range(2):
                        w = [wo_ld]
                        if dk == 0 and ts2 == 0 and nh == 0:
                            w.append((s_ev, oc_ready))
                            if tp == 1:
                                w.append((s_ve, s_ve.n))
                        last = (ts2 == 1 and nh == 1)
                        P.on("tensor", lambda e, dk=dk, tsub=tsub, nh=nh,
                             ts2=ts2, wp=wp: e.matmul(
                                 s4[:, ts2 * 1024 + nh * 512:
                                    ts2 * 1024 + nh * 512 + 512],
                                 oc_sb[:, dk * 512 + tsub * 128:
                                       dk * 512 + tsub * 128 + 128],
                                 wob[:, wp * 1024 + nh * 512:
                                     wp * 1024 + nh * 512 + 512],
                                 start=(dk == 0), stop=False),
                             waits=tuple(w), inc=(s_wo if last else None))
            for ts2 in range(2):
                for nh in range(2):
                    P.on("tensor", lambda e, nh=nh, ts2=ts2: e.matmul(
                        s4[:, ts2 * 1024 + nh * 512: ts2 * 1024 + nh * 512 + 512],
                        onesf[0:1, :], borow[0:1, nh * 512:(nh + 1) * 512],
                        start=False, stop=True), inc=(s_te if nh == 1 else None))
            pfence()
            wo_mm = s_te.n
            for ts2 in range(2):
                tsub = tp * 2 + ts2
                ti = q * 4 + tsub
                P.on("sync", lambda e, ti=ti, ts2=ts2: ld(
                    e, xtile[:, ts2 * 1024:(ts2 + 1) * 1024],
                    x_in.ap()[ti * 128:(ti + 1) * 128, :]),
                    waits=((s_ve, s_ve.n),), inc=lnx[ts2], inc_k=16)
                x_ld = (lnx[ts2], lnx[ts2].n)
                P.on("vector", lambda e, ts2=ts2: e.tensor_add(
                    x2stg[:, ts2 * 1024:(ts2 + 1) * 1024],
                    s4[:, ts2 * 1024:ts2 * 1024 + 1024],
                    xtile[:, ts2 * 1024:(ts2 + 1) * 1024]),
                    waits=((s_te, wo_mm), x_ld, (s_r1h[ts2], s_r1h[ts2].n)),
                    inc=s_ve)
                st_t = s_ve.n
                P.on("gpsimd", lambda e, ti=ti, ts2=ts2: ld(
                    e, r1.ap()[ti * 128:(ti + 1) * 128, :],
                    x2stg[:, ts2 * 1024:(ts2 + 1) * 1024]),
                    waits=((s_ve, st_t),), inc=s_r1h[ts2], inc_k=16)

    emit_toep(len(toep_dmas))

    wfft = A.at("wfft", 16384, [128, 8 * 1024], BF16)
    fxt = A.at("fxt", 32768, [128, 2 * 1024], F32)
    fsq = A.at("fsq", 40960, [128, 2 * 1024], F32)
    fxn = A.at("fxn", 49152, [128, 2 * 1024], F32)
    ftr = A.at("ftr", 57344, [128, 2 * 1024], F32)
    fxnc = A.at("fxnc", 65536, [128, 2 * 4096], F32)
    fxq = A.at("fxq", 98304, [128, 2 * 4096], BF16)
    ffc = A.at("ffc", 114688, [128, 2 * 4096], F32)
    fx2 = A.at("fx2", 147456, [128, 2 * 1024], F32)
    fx3 = A.at("fx3", 155648, [128, 2 * 1024], F32)
    w8stg = A.at("w8stg", 163840, [128, 8 * 1024], I8)
    w8bf = A.at("w8bf", 172032, [128, 1024], BF16)
    fym = A.at("fym", 174080, [128, 4096], F32)
    fevt = A.at("fevt", 190464, [128, 2 * 512], F32)
    ccin = dt_("ccin", [1, 1], F32)
    ccout = dt_("ccout", [1, 1], F32)
    wfft_built = [0]

    def emit_wfft():
        P.on("sync", lambda e: ld(
            e, w8stg[:], _ap(ffn_w8, 0, [[1024, 128], [128 * 1024, 8], [1, 1024]])),
            waits=((s_ve, s_ve.n), (s_te, s_te.n)), inc=kstsm, inc_k=16)
        wld = (kstsm, kstsm.n)
        for ncb in range(8):
            P.on("vector", lambda e, ncb=ncb: e.tensor_copy(
                w8bf[:], w8stg[:, ncb * 1024:(ncb + 1) * 1024]),
                waits=(wld, (s_ve, s_ve.n), (s_te, s_te.n)), inc=s_ve)
            cst = s_ve.n
            P.on("tensor", lambda e: [e.transpose(
                pb[:, kc * 128:(kc + 1) * 128], w8bf[:, kc * 128:(kc + 1) * 128],
                ident_sb[:]) for kc in range(8)][-1],
                waits=((s_ve, cst),), inc=None)
            tfence()
            P.on("vector", lambda e, ncb=ncb: e.tensor_copy(
                _ap(wfft, ncb * 128, [[8 * 1024, 128], [1024, 8], [1, 128]]),
                pb[:]), waits=(s_te,), inc=s_ve)
        wfft_built[0] = s_ve.n

    def emit_ffn(src, dst, xnc, final, src_w=()):
        f_cc = P.sem()
        cons = [0, 0]
        sqr = [0, 0]
        ev_prev = [0]
        tp_prev = [0]
        for i in range(NT):
            par = i % 2
            sm = i % 2
            w = ((s_ve, cons[par]),) if i >= 2 else (
                (s_ve, s_ve.n), (s_te, s_te.n), (s_ac, s_ac.n)) + tuple(src_w)
            P.on("sync", lambda e, i=i, par=par: ld(
                e, fxt[:, par * 1024:(par + 1) * 1024],
                src.ap()[i * 128:(i + 1) * 128, :]), waits=w, inc=lnx[par], inc_k=16)
            xld = (lnx[par], lnx[par].n)
            xt = fxt[:, par * 1024:(par + 1) * 1024]
            P.on("vector", lambda e, xt=xt, sm=sm: e.reduce_sum(
                sml[:, sm:sm + 1], xt, axis=AX), waits=(xld,), inc=s_ve)
            P.on("scalar", lambda e, xt=xt, par=par: e.activation(
                fsq[:, par * 1024:(par + 1) * 1024], xt, AF.Square),
                waits=(xld, (s_ve, sqr[par])), inc=s_ac)
            P.on("vector", lambda e, sm=sm, par=par: e.reduce_sum(
                sml[:, 2 + sm:3 + sm], fsq[:, par * 1024:(par + 1) * 1024], axis=AX),
                waits=(s_ac,), inc=s_ve)
            sqr[par] = s_ve.n
            P.on("vector", lambda e, sm=sm: e.tensor_scalar(
                sml[:, 4 + sm:5 + sm], sml[:, sm:sm + 1], -1.0 / D, None, ALU.mult),
                waits=(s_ve,), inc=s_ve)
            P.on("vector", lambda e, sm=sm: e.tensor_mul(
                sml[:, 6 + sm:7 + sm], sml[:, sm:sm + 1], sml[:, sm:sm + 1]),
                waits=(s_ve,), inc=s_ve)
            P.on("vector", lambda e, sm=sm: e.tensor_scalar(
                sml[:, 8 + sm:9 + sm], sml[:, 2 + sm:3 + sm], 1.0 / D, EPS,
                ALU.mult, ALU.add), waits=(s_ve,), inc=s_ve)
            P.on("vector", lambda e, sm=sm: e.scalar_tensor_tensor(
                sml[:, 10 + sm:11 + sm], sml[:, 6 + sm:7 + sm], -1.0 / (D * D),
                sml[:, 8 + sm:9 + sm], ALU.mult, ALU.add), waits=(s_ve,), inc=s_ve)
            P.on("scalar", lambda e, sm=sm: e.activation(
                sml[:, 12 + sm:13 + sm], sml[:, 10 + sm:11 + sm], AF.Ln),
                waits=(s_ve,), inc=s_ac)
            P.on("scalar", lambda e, sm=sm: e.activation(
                sml[:, 14 + sm:15 + sm], sml[:, 12 + sm:13 + sm], AF.Exp, scale=-0.5),
                waits=(s_ac,), inc=s_ac)
            P.on("vector", lambda e, xt=xt, sm=sm, par=par: e.tensor_scalar(
                fxn[:, par * 1024:(par + 1) * 1024], xt,
                sml[:, 4 + sm:5 + sm], sml[:, 14 + sm:15 + sm], ALU.add, ALU.mult),
                waits=((s_ac, s_ac.n), (s_te, tp_prev[0])), inc=s_ve)
            nrm = s_ve.n
            cons[par] = s_ve.n
            P.on("vector", lambda e, i=i, par=par: e.reduce_max(
                maxs_sb[:, i:i + 1], fxn[:, par * 1024:(par + 1) * 1024], axis=AX,
                apply_absolute_value=True), waits=((s_ve, nrm),), inc=s_ve)
            P.on("tensor", lambda e, par=par: [e.transpose(
                _ap(o2, k * 128, [[1024, 128], [1, 128]]),
                fxn[:, par * 1024 + k * 128: par * 1024 + (k + 1) * 128],
                identf_sb[:]) for k in range(8)][-1],
                waits=((s_ve, s_ve.n), (s_ve, ev_prev[0])), inc=None)
            tfence()
            tp_prev[0] = s_te.n
            P.on("vector", lambda e, par=par: e.tensor_copy(
                ftr[:, par * 1024:(par + 1) * 1024], o2[:]),
                waits=(s_te, (s_xch[par], s_xch[par].n)), inc=s_ve)
            ev_prev[0] = s_ve.n
            P.on("gpsimd", lambda e, i=i, par=par: ld(
                e, _ap(xnc, i * 128, [[T, 128], [128 * T, 8], [1, 128]]),
                ftr[:, par * 1024:(par + 1) * 1024]),
                waits=((s_ve, ev_prev[0]),), inc=s_xch[par], inc_k=16)
        ln_done = s_ve.n
        P.on("vector", lambda e: e.reduce_max(
            sml[:, 16:17], maxs_sb[:], axis=AX), waits=((s_ve, ln_done),), inc=s_ve)
        P.on("vector", lambda e: e.memset(ftr[:, 0:128], 0.0),
             waits=((s_xch[0], s_xch[0].n), (s_xch[1], s_xch[1].n)), inc=s_ve)
        P.on("vector", lambda e: e.tensor_copy(ftr[:, 0:1], sml[:, 16:17]),
             waits=(s_ve,), inc=s_ve)
        mx = s_ve.n
        P.on("tensor", lambda e: e.transpose(
            _ap(o2, 0, [[1024, 128], [1, 128]]), ftr[:, 0:128], identf_sb[:]),
            waits=((s_ve, mx), (s_ve, s_ve.n)), inc=None)
        tfence()
        P.on("vector", lambda e: e.reduce_max(
            sml[0:1, 17:18], _ap(o2, 0, [[1024, 1], [1, 128]]), axis=AX),
            waits=(s_te,), inc=s_ve)
        P.on("vector", lambda e: e.tensor_copy(ftr[0:1, 0:1], sml[0:1, 17:18]),
             waits=(s_ve,), inc=s_ve)
        P.on("gpsimd", lambda e: ld(e, ccin.ap(), ftr[0:1, 0:1]),
             waits=((s_ve, s_ve.n),), inc=s_ccio, inc_k=16)
        P.on("gpsimd", lambda e: e.collective_compute(
            "AllReduce", ALU.max, replica_groups=[list(range(8))],
            ins=[ccin.ap()], outs=[ccout.ap()]),
            waits=((s_ccio, s_ccio.n),), inc=f_cc)
        P.on("gpsimd", lambda e: ld(e, ftr[0:1, 0:1], ccout.ap()),
             waits=((f_cc, 1),), inc=s_ccio, inc_k=16)
        ccd = s_ccio.n
        P.on("vector", lambda e: e.tensor_scalar(
            sml[0:1, 18:19], ftr[0:1, 0:1], 1.0 / 127.0, EPS, ALU.mult, ALU.max),
            waits=((s_ccio, ccd),), inc=s_ve)
        sc = s_ve.n
        if DEBUG in ("sc", "att1") or DEBUG is True:
            fi = 0 if src is r1 else 1
            for j, sl in ((0, sml[0:1, 17:18]), (1, ftr[0:1, 0:1]),
                          (2, sml[0:1, 18:19])):
                P.on("gpsimd", lambda e, fi=fi, j=j, sl=sl: ld(
                    e, _ap(dbg["sc"], fi * 4 + j, [[8, 1], [1, 1]]), sl),
                    waits=((s_ve, sc),), inc=s_scd, inc_k=16)
        P.on("tensor", lambda e: e.matmul(
            _ap(aux, 0, [[512, 128], [1, 1]]), onesf[0:1, :], sml[0:1, 18:19],
            start=True, stop=True), waits=((s_ve, sc),), inc=s_te)
        pfence(out=_ap(aux, 1, [[512, 1], [1, 1]]))
        P.on("vector", lambda e: e.tensor_copy(
            sml[:, 20:21], _ap(aux, 0, [[512, 128], [1, 1]])),
            waits=(s_te,), inc=s_ve)
        P.on("vector", lambda e: e.reciprocal(invs_sb[:], sml[:, 20:21]),
             waits=(s_ve,), inc=s_ve)
        P.on("vector", lambda e: e.tensor_scalar(
            ssw_sb[:], wsc_sb[:], sml[:, 20:21], None, ALU.mult),
            waits=(s_ve,), inc=s_ve)
        scales_v = s_ve.n
        bank_fr = [0, 0, 0, 0]
        evtf = [0, 0]
        qprev = [0, 0]
        qacc = [0, 0]
        ymf = [0]
        addp = [0]
        for ts in range(NQ):
            par = ts % 2
            w = [(s_xch[0], s_xch[0].n), (s_xch[1], s_xch[1].n)]
            if qacc[par] > 0:
                w.append((s_ac, qacc[par]))
            P.on("sync", lambda e, ts=ts, par=par: ld(
                e, fxnc[:, par * 4096:(par + 1) * 4096],
                _ap(xnc, ts * 512, [[T, 128], [128 * T, 8], [1, 512]])),
                waits=tuple(w), inc=fxnsm[par], inc_k=16)
            cld = (fxnsm[par], fxnsm[par].n)
            P.on("scalar", lambda e, par=par: e.activation(
                fym[:], fxnc[:, par * 4096:(par + 1) * 4096], AF.Identity,
                bias=mgc_sb[:], scale=invs_sb[:]),
                waits=(cld, (s_ve, scales_v), (s_ve, ymf[0])), inc=s_ac)
            qacc[par] = s_ac.n
            P.on("vector", lambda e, par=par: e.tensor_scalar(
                fxq[:, par * 4096:(par + 1) * 4096], fym[:], -MAGIC, None, ALU.add),
                waits=((s_ac, s_ac.n), (s_te, qprev[par])), inc=s_ve)
            ymf[0] = s_ve.n
            qrdy = s_ve.n
            for ncb in range(8):
                bank = ncb % 4
                for kc in range(8):
                    w = [(s_ve, qrdy), (s_ve, wfft_built[0])]
                    if kc == 0 and bank_fr[bank] > 0:
                        w.append((s_ac, bank_fr[bank]))
                    P.on("tensor", lambda e, ncb=ncb, kc=kc, bank=bank, par=par:
                         e.matmul(
                             s4[:, bank * 512:(bank + 1) * 512],
                             wfft[:, kc * 1024 + ncb * 128: kc * 1024 + ncb * 128 + 128],
                             fxq[:, par * 4096 + kc * 512: par * 4096 + (kc + 1) * 512],
                             start=(kc == 0), stop=(kc == 7)),
                         waits=tuple(w) if kc == 0 else (),
                         inc=(s_te if kc == 7 else None))
                ep = ncb % 2
                pfence()
                P.on("scalar", lambda e, bank=bank, ep=ep: e.activation(
                    fevt[:, ep * 512:(ep + 1) * 512],
                    s4[:, bank * 512:(bank + 1) * 512], AF.Identity),
                    waits=((s_te, s_te.n), (s_ve, evtf[ep])), inc=s_ac)
                bank_fr[bank] = s_ac.n
                P.on("vector", lambda e, ncb=ncb, par=par, ep=ep: e.tensor_scalar(
                    ffc[:, par * 4096 + ncb * 512: par * 4096 + (ncb + 1) * 512],
                    fevt[:, ep * 512:(ep + 1) * 512],
                    ssw_sb[:, ncb:ncb + 1], bff_sb[:, ncb:ncb + 1],
                    ALU.mult, ALU.add),
                    waits=((s_ac, s_ac.n),), inc=s_ve)
                evtf[ep] = s_ve.n
            qprev[par] = s_te.n
            evd_v = s_ve.n
            for tsub in range(4):
                sp = tsub % 2
                ti = ts * 4 + tsub
                P.on("sync", lambda e, ti=ti, sp=sp: ld(
                    e, fx2[:, sp * 1024:(sp + 1) * 1024],
                    src.ap()[ti * 128:(ti + 1) * 128, :]),
                    waits=((s_ve, addp[0]),), inc=fx2sm[sp], inc_k=16)
                xld2 = (fx2sm[sp], fx2sm[sp].n)
                P.on("tensor", lambda e, tsub=tsub, par=par: [e.transpose(
                    _ap(o2, k * 128, [[1024, 128], [1, 128]]),
                    ffc[:, par * 4096 + k * 512 + tsub * 128:
                        par * 4096 + k * 512 + tsub * 128 + 128],
                    identf_sb[:]) for k in range(8)][-1],
                    waits=((s_ve, evd_v), (s_ve, addp[0])), inc=None)
                tfence()
                trd = s_te.n
                P.on("vector", lambda e, sp=sp: e.tensor_add(
                    fx3[:, sp * 1024:(sp + 1) * 1024], o2[:],
                    fx2[:, sp * 1024:(sp + 1) * 1024]),
                    waits=((s_te, trd), xld2, (s_dsth[sp], s_dsth[sp].n)), inc=s_ve)
                addp[0] = s_ve.n
                P.on("gpsimd", lambda e, ti=ti, sp=sp: ld(
                    e, dst.ap()[ti * 128:(ti + 1) * 128, :],
                    fx3[:, sp * 1024:(sp + 1) * 1024]),
                    waits=((s_ve, addp[0]),), inc=s_dsth[sp], inc_k=16)
        if final:
            fw = ((s_dsth[0], s_dsth[0].n), (s_dsth[1], s_dsth[1].n))
            if DEBUG in ("sc", "att1") or DEBUG is True:
                fw = fw + ((s_scd, s_scd.n),)
            P.on("sync", lambda e: None, waits=fw)

    xn2c = dt_("xn2c", [ND, 128, T], F32)
    if (DEBUG in ("sc", "att1") or DEBUG is True) and "ffn1" in PHASES:
        P.on("gpsimd", lambda e: ld(e, dbg["r1"].ap(), r1.ap()),
             waits=((s_r1h[0], s_r1h[0].n), (s_r1h[1], s_r1h[1].n)),
             inc=s_scd, inc_k=16)
    if "ffn1" in PHASES:
        emit_wfft()
        emit_ffn(r1, r2 if "attnb" in PHASES else out_t, xn2c,
                 final=("attnb" not in PHASES),
                 src_w=((s_r1h[0], s_r1h[0].n), (s_r1h[1], s_r1h[1].n)))

    winp = A.at("winp", 32768, [128, 8 * 1024], BF16)
    v_t = A.at("v_t", 49152, [128, 1024 * 33], BF16)
    x1_t = A.at("x1_t", 116736, [128, 1024 * 33], BF16)
    bxnbb = A.at("bxnbb", 184320, [128, 2 * 1024], BF16)
    tpb = A.at("tpb", 188416, [128, 8 * 256], BF16)
    binrow = A.at("binrow", 192512, [1, 1024], F32)
    cvsf = A.at("cvsf", 192512, [128, 512], F32)
    cvs2 = A.at("cvs2", 194560, [128, 512], BF16)
    bxnb = A.at("bxnb", 98304, [128, 1024], BF16)
    bxnbt = A.at("bxnbt", 100352, [128, 1024], BF16)
    wout_sb = winp
    y2c = A.at("y2c", 116736, [128, 2 * 4096], BF16)
    woc = A.at("woc", 133120, [128, 2 * 4096], F32)
    z3rd = A.at("z3rd", 165888, [128, 2 * 1024], F32)
    x4st = A.at("x4st", 174080, [128, 2 * 1024], F32)
    bout_row = A.at("bout_row", 184320, [1, D], F32)
    ones512b = A.at("ones512b", 188416, [1, 512], F32)

    def emit_attnb(src, dst, final, src_w=()):
        b_cc = P.sem()
        cons = [0, 0]
        sqr = [0, 0]
        tp_prev = [0]
        for i in range(NT):
            par = i % 2
            sm = i % 2
            w = ((s_ve, cons[par]),) if i >= 2 else (
                (s_ve, s_ve.n), (s_te, s_te.n), (s_ac, s_ac.n)) + tuple(src_w)
            P.on("sync", lambda e, i=i, par=par: ld(
                e, fxt[:, par * 1024:(par + 1) * 1024],
                src.ap()[i * 128:(i + 1) * 128, :]), waits=w, inc=lnx[par], inc_k=16)
            xld = (lnx[par], lnx[par].n)
            xt = fxt[:, par * 1024:(par + 1) * 1024]
            P.on("vector", lambda e, xt=xt, sm=sm: e.reduce_sum(
                sml[:, sm:sm + 1], xt, axis=AX), waits=(xld,), inc=s_ve)
            P.on("scalar", lambda e, xt=xt, par=par: e.activation(
                fsq[:, par * 1024:(par + 1) * 1024], xt, AF.Square),
                waits=(xld, (s_ve, sqr[par])), inc=s_ac)
            P.on("vector", lambda e, sm=sm, par=par: e.reduce_sum(
                sml[:, 2 + sm:3 + sm], fsq[:, par * 1024:(par + 1) * 1024], axis=AX),
                waits=(s_ac,), inc=s_ve)
            sqr[par] = s_ve.n
            P.on("vector", lambda e, sm=sm: e.tensor_scalar(
                sml[:, 4 + sm:5 + sm], sml[:, sm:sm + 1], -1.0 / D, None, ALU.mult),
                waits=(s_ve,), inc=s_ve)
            P.on("vector", lambda e, sm=sm: e.tensor_mul(
                sml[:, 6 + sm:7 + sm], sml[:, sm:sm + 1], sml[:, sm:sm + 1]),
                waits=(s_ve,), inc=s_ve)
            P.on("vector", lambda e, sm=sm: e.tensor_scalar(
                sml[:, 8 + sm:9 + sm], sml[:, 2 + sm:3 + sm], 1.0 / D, EPS,
                ALU.mult, ALU.add), waits=(s_ve,), inc=s_ve)
            P.on("vector", lambda e, sm=sm: e.scalar_tensor_tensor(
                sml[:, 10 + sm:11 + sm], sml[:, 6 + sm:7 + sm], -1.0 / (D * D),
                sml[:, 8 + sm:9 + sm], ALU.mult, ALU.add), waits=(s_ve,), inc=s_ve)
            P.on("scalar", lambda e, sm=sm: e.activation(
                sml[:, 12 + sm:13 + sm], sml[:, 10 + sm:11 + sm], AF.Ln),
                waits=(s_ve,), inc=s_ac)
            P.on("scalar", lambda e, sm=sm: e.activation(
                sml[:, 14 + sm:15 + sm], sml[:, 12 + sm:13 + sm], AF.Exp, scale=-0.5),
                waits=(s_ac,), inc=s_ac)
            P.on("vector", lambda e, xt=xt, sm=sm, par=par: e.tensor_scalar(
                fxn[:, par * 1024:(par + 1) * 1024], xt,
                sml[:, 4 + sm:5 + sm], sml[:, 14 + sm:15 + sm], ALU.add, ALU.mult),
                waits=((s_ac, s_ac.n),), inc=s_ve)
            nrm = s_ve.n
            xnt = fxn[:, par * 1024:(par + 1) * 1024]
            P.on("vector", lambda e, xt=xt, xnt=xnt, par=par: e.tensor_add(
                fx3[:, par * 1024:(par + 1) * 1024], xnt, xt),
                waits=((s_ve, nrm), (s_z3h[par], s_z3h[par].n)), inc=s_ve)
            cons[par] = s_ve.n
            P.on("gpsimd", lambda e, i=i, par=par: ld(
                e, z3.ap()[i * 128:(i + 1) * 128, :],
                fx3[:, par * 1024:(par + 1) * 1024]),
                waits=((s_ve, s_ve.n),), inc=s_z3h[par], inc_k=16)
            P.on("vector", lambda e, xnt=xnt, sm=sm: e.reduce_sum(
                sml[:, 16 + sm:17 + sm], xnt, axis=AX), waits=((s_ve, nrm),), inc=s_ve)
            P.on("vector", lambda e, xnt=xnt, sm=sm: e.scalar_tensor_tensor(
                ftr[:, (sm) * 1024:(sm + 1) * 1024], xnt, 1.0, xnt,
                ALU.mult, ALU.mult, accum_out=sml[:, 18 + sm:19 + sm]),
                waits=((s_ve, s_ve.n), (s_ac, s_ac.n)), inc=s_ve)
            P.on("vector", lambda e, sm=sm: e.tensor_scalar(
                sml[:, 4 + sm:5 + sm], sml[:, 16 + sm:17 + sm], -1.0 / D, None,
                ALU.mult), waits=(s_ve,), inc=s_ve)
            P.on("vector", lambda e, sm=sm: e.tensor_mul(
                sml[:, 6 + sm:7 + sm], sml[:, 16 + sm:17 + sm],
                sml[:, 16 + sm:17 + sm]), waits=(s_ve,), inc=s_ve)
            P.on("vector", lambda e, sm=sm: e.tensor_scalar(
                sml[:, 8 + sm:9 + sm], sml[:, 18 + sm:19 + sm], 1.0 / D, EPS,
                ALU.mult, ALU.add), waits=(s_ve,), inc=s_ve)
            P.on("vector", lambda e, sm=sm: e.scalar_tensor_tensor(
                sml[:, 10 + sm:11 + sm], sml[:, 6 + sm:7 + sm], -1.0 / (D * D),
                sml[:, 8 + sm:9 + sm], ALU.mult, ALU.add), waits=(s_ve,), inc=s_ve)
            P.on("scalar", lambda e, sm=sm: e.activation(
                sml[:, 12 + sm:13 + sm], sml[:, 10 + sm:11 + sm], AF.Ln),
                waits=(s_ve,), inc=s_ac)
            P.on("scalar", lambda e, sm=sm: e.activation(
                sml[:, 14 + sm:15 + sm], sml[:, 12 + sm:13 + sm], AF.Exp,
                scale=-0.5), waits=(s_ac,), inc=s_ac)
            P.on("vector", lambda e, xnt=xnt, sm=sm: e.tensor_scalar(
                bxnb[:], xnt, sml[:, 4 + sm:5 + sm], sml[:, 14 + sm:15 + sm],
                ALU.add, ALU.mult),
                waits=((s_ac, s_ac.n), (s_te, tp_prev[0])), inc=s_ve)
            nb = s_ve.n
            P.on("tensor", lambda e: [e.transpose(
                pb[:, k * 128:(k + 1) * 128], bxnb[:, k * 128:(k + 1) * 128],
                ident_sb[:]) for k in range(8)][-1],
                waits=((s_ve, nb), (s_ve, s_ve.n)), inc=None)
            tfence()
            tp_prev[0] = s_te.n
            P.on("vector", lambda e: e.tensor_copy(bxnbt[:], pb[:]),
                 waits=(s_te, (s_xbst, s_xbst.n)), inc=s_ve)
            bev = s_ve.n
            P.on("gpsimd", lambda e, i=i: ld(
                e, _ap(xnbc, i * 128 * (ND * 128), [[ND * 128, 128], [1, ND * 128]]),
                bxnbt[:]), waits=((s_ve, bev),), inc=s_xbst, inc_k=16)
        ln3_done = s_xbst.n

        wpstg = A.at("wpstg", 133120, [128, 8 * 1024], F32)

        def w_part(part, dst_t, gate_conv):
            poff = part * 1024
            P.on("sync", lambda e: ld(
                e, binrow[:], _ap(b_in, poff, [[3072, 1], [1, 1024]])),
                waits=((s_ve, s_ve.n), (s_te, s_te.n)), inc=s_dma, inc_k=16)
            bld = s_dma.n
            P.on("sync", lambda e: ld(
                e, wpstg[:], _ap(W_in, poff, [[3072, 128], [128 * 3072, 8],
                                              [1, 1024]])),
                waits=((s_ac, s_ac.n), (s_te, s_te.n)), inc=wstsm, inc_k=16)
            wld = (wstsm, wstsm.n)
            for dk in range(ND):
                P.on("scalar", lambda e, dk=dk: e.activation(
                    winp[:, dk * 1024:(dk + 1) * 1024],
                    wpstg[:, dk * 1024:(dk + 1) * 1024],
                    AF.Identity, scale=gam_sb[:, dk:dk + 1]),
                    waits=(wld,), inc=s_ac)
            bwv = [0]
            for cp in range(4):
                for dk in range(ND):
                    aw = [wld]
                    if dk == 0 and bwv[0] > 0:
                        aw.append((s_ve, bwv[0]))
                    P.on("tensor", lambda e, cp=cp, dk=dk: e.matmul(
                        _ap(aux, (cp % 2) * 256, [[512, 1], [1, 256]]),
                        beta_sb[:, dk:dk + 1],
                        wpstg[:, dk * 1024 + cp * 256: dk * 1024 + cp * 256 + 256],
                        start=(dk == 0), stop=(dk == ND - 1)),
                        waits=tuple(aw) if dk == 0 else (),
                        inc=(s_te if dk == ND - 1 else None))
                pfence(out=s4[0:1, 1024:1025])
                P.on("vector", lambda e, cp=cp: e.tensor_add(
                    binrow[0:1, cp * 256:(cp + 1) * 256],
                    binrow[0:1, cp * 256:(cp + 1) * 256],
                    _ap(aux, (cp % 2) * 256, [[512, 1], [1, 256]])),
                    waits=((s_te, s_te.n), (s_dma, bld)), inc=s_ve)
                bwv[0] = s_ve.n
            wrdy_ac = s_ac.n
            wrdy_ve = s_ve.n
            ev_pr = [0, 0]
            xb_pr = [0, 0]
            for i in range(NT):
                par = i % 2
                w = [(s_xbst, ln3_done)]
                if xb_pr[par] > 0:
                    w.append((s_te, xb_pr[par]))
                P.on("sync", lambda e, i=i, par=par: ld(
                    e, bxnbb[:, par * 1024:(par + 1) * 1024],
                    _ap(xnbc, i * 128 * (ND * 128), [[ND * 128, 128], [1, ND * 128]])),
                    waits=tuple(w), inc=bxbsm[par], inc_k=16)
                xbl = (bxbsm[par], bxbsm[par].n)
                for nc2 in range(2):
                    wm = [xbl, (s_ac, wrdy_ac), (s_ve, wrdy_ve)]
                    if ev_pr[nc2] > 0:
                        wm.append((s_ve, ev_pr[nc2]))
                    for dk in range(ND):
                        P.on("tensor", lambda e, i=i, par=par, nc2=nc2, dk=dk:
                             e.matmul(
                                 s4[:, nc2 * 512:(nc2 + 1) * 512],
                                 bxnbb[:, par * 1024 + dk * 128:
                                       par * 1024 + (dk + 1) * 128],
                                 winp[:, dk * 1024 + nc2 * 512:
                                      dk * 1024 + nc2 * 512 + 512],
                                 start=(dk == 0), stop=False),
                             waits=tuple(wm) if dk == 0 else (), inc=None)
                    P.on("tensor", lambda e, nc2=nc2: e.matmul(
                        s4[:, nc2 * 512:(nc2 + 1) * 512], onesf[0:1, :],
                        binrow[0:1, nc2 * 512:(nc2 + 1) * 512],
                        start=False, stop=True), inc=s_te)
                    pfence(out=s4[0:1, 1024:1025])
                    if nc2 == 1:
                        xb_pr[par] = s_te.n
                    mmv = s_te.n
                    P.on("vector", lambda e, i=i, nc2=nc2, dst_t=dst_t:
                         e.tensor_copy(
                             _ap(dst_t, (nc2 * 512) * 33 + i + 1,
                                 [[1024 * 33, 128], [33, 512]]),
                             s4[:, nc2 * 512:(nc2 + 1) * 512]),
                         waits=((s_te, mmv),), inc=s_ve)
                    ev_pr[nc2] = s_ve.n
            return s_ve.n

        P.on("vector", lambda e: e.memset(
            _ap(v_t, 0, [[1024 * 33, 128], [33, 1024]]), 0.0),
            waits=((s_ve, s_ve.n), (s_te, s_te.n)), inc=s_ve)
        if DEBUG == "attnb":
            P.on("gpsimd", lambda e: ld(e, dbg["xnbc"].ap(), xnbc.ap()),
                 waits=((s_xbst, ln3_done),), inc=s_sto, inc_k=16)
            P.on("gpsimd", lambda e: ld(e, dbg["z3"].ap(), z3.ap()),
                 waits=((s_xbst, ln3_done),), inc=s_sto, inc_k=16)
        w_part(0, v_t, None)
        if DEBUG == "attnb":
            P.on("gpsimd", lambda e: ld(e, dbg["vt"].ap(), v_t[:]),
                 waits=((s_ve, s_ve.n),), inc=s_sto, inc_k=16)
        w_part(1, x1_t, None)
        if DEBUG == "attnb":
            P.on("gpsimd", lambda e: ld(e, dbg["x1t"].ap(), x1_t[:]),
                 waits=((s_ve, s_ve.n),), inc=s_sto, inc_k=16)

        def conv_pass(order, ysrc, gsrc, toep_off):
            tp_ld = [0] * 8
            tp_use = [0] * 8
            evA = [0]
            evB = [s_ve.n]
            gdone = [s_ve.n]
            cvs_te = [0]
            for g16 in range(64):
                for rnd in range(4):
                    for cc in range(4):
                        ch = g16 * 16 + rnd * 4 + cc
                        slot = ch % 8
                        w = []
                        if tp_use[slot] > 0:
                            w.append((s_te, tp_use[slot]))
                        if g16 == 0 and rnd == 0 and cc == 0:
                            w.append((s_tp, s_tp.n))
                        P.on("sync", lambda e, ch=ch, slot=slot: ld(
                            e, tpb[:, slot * 256:(slot + 1) * 256],
                            _ap(toep, (toep_off + ch) * 128 * 256,
                                [[256, 128], [1, 256]])),
                            waits=tuple(w), inc=tpsm[slot], inc_k=16)
                        tp_ld[slot] = (tpsm[slot], tpsm[slot].n)
                        wm = [tp_ld[slot]]
                        if rnd == 0 and cc == 0:
                            wm.append((s_ve, evB[0]))
                            if evA[0] > 0:
                                wm.append((s_ac, evA[0]))
                        P.on("tensor", lambda e, ch=ch, slot=slot, rnd=rnd, cc=cc,
                             ysrc=ysrc: e.matmul(
                                 _ap(o2, (cc * 32) * 1024 + rnd * 128,
                                     [[1024, 32], [1, 128]]),
                                 _ap(ysrc, ch * 33 + 1, [[1024 * 33, 128], [1, 32]]),
                                 tpb[:, slot * 256: slot * 256 + 128],
                                 start=True, stop=True,
                                 tile_position=(0, cc * 32)),
                             waits=tuple(wm), inc=None)
                        P.on("tensor", lambda e, ch=ch, slot=slot, rnd=rnd, cc=cc,
                             ysrc=ysrc: e.matmul(
                                 _ap(o2, (cc * 32) * 1024 + 512 + rnd * 128,
                                     [[1024, 32], [1, 128]]),
                                 _ap(ysrc, ch * 33, [[1024 * 33, 128], [1, 32]]),
                                 tpb[:, slot * 256 + 128: slot * 256 + 256],
                                 start=True, stop=True,
                                 tile_position=(0, cc * 32)),
                             inc=s_te)
                        tp_use[slot] = s_te.n
                        if CONV_SERIAL:
                            P.on("tensor", lambda e: None,
                                 waits=((s_te, s_te.n),))
                pfence(out=s4[0:1, 0:1])
                mm16 = s_te.n
                P.on("scalar", lambda e: e.activation(
                    cvsf[:], o2[:, 0:512], AF.Identity),
                    waits=((s_te, mm16), (s_te, cvs_te[0]), (s_ve, s_ve.n)), inc=s_ac)
                evA[0] = s_ac.n
                P.on("vector", lambda e: e.tensor_add(
                    cvs2[:], cvsf[:], o2[:, 512:1024]),
                    waits=((s_ac, evA[0]), (s_te, cvs_te[0])), inc=s_ve)
                evB[0] = s_ve.n
                P.on("tensor", lambda e: [e.transpose(
                    pb[:, r * 128:(r + 1) * 128], cvs2[:, r * 128:(r + 1) * 128],
                    ident_sb[:]) for r in range(4)][-1],
                    waits=((s_ve, evB[0]), (s_ve, gdone[0])), inc=None)
                tfence()
                cvs_te[0] = s_te.n
                P.on("vector", lambda e, g16=g16, gsrc=gsrc, ysrc=ysrc: e.tensor_mul(
                    _ap(ysrc, (g16 * 16) * 33 + 1,
                        [[1024 * 33, 128], [132, 4], [33, 4], [1, 32]]),
                    _ap(pb, 0, [[1024, 128], [128, 4], [32, 4], [1, 32]]),
                    _ap(gsrc, (g16 * 16) * 33 + 1,
                        [[1024 * 33, 128], [132, 4], [33, 4], [1, 32]])),
                    waits=((s_te, cvs_te[0]),), inc=s_ve)
                gdone[0] = s_ve.n
            return s_ve.n

        conv_pass(0, v_t, x1_t, 0)
        if DEBUG == "attnb":
            P.on("gpsimd", lambda e: ld(e, dbg["y1"].ap(), v_t[:]),
                 waits=((s_ve, s_ve.n),), inc=s_sto, inc_k=16)
        w_part(2, x1_t, None)
        if DEBUG == "attnb":
            P.on("gpsimd", lambda e: ld(e, dbg["x2t"].ap(), x1_t[:]),
                 waits=((s_ve, s_ve.n),), inc=s_sto, inc_k=16)
        conv_pass(1, v_t, x1_t, D)
        if DEBUG == "attnb":
            P.on("gpsimd", lambda e: ld(e, dbg["y2"].ap(), v_t[:]),
                 waits=((s_ve, s_ve.n),), inc=s_sto, inc_k=16)

        P.on("sync", lambda e: ld(e, bout_row[:], _ap(b_out, 0, [[1024, 1], [1, 1024]])),
             waits=((s_ve, s_ve.n), (s_te, s_te.n)), inc=s_dma, inc_k=16)
        bo_ld = s_dma.n
        P.on("vector", lambda e: e.memset(ones512b[:], 1.0),
             waits=((s_te, s_te.n), (s_dma, s_dma.n)), inc=s_ve)
        o5_v = s_ve.n
        wostg = A.at("wostg", 16384, [128, 4 * 1024], F32)
        for h in range(2):
            P.on("sync", lambda e, h=h: ld(
                e, wostg[:], _ap(W_out, (h * 4 * 128) * D,
                                 [[D, 128], [128 * D, 4], [1, 1024]])),
                waits=((s_ac, s_ac.n), (s_ve, s_ve.n)), inc=wstsm, inc_k=16)
            P.on("vector", lambda e, h=h: e.tensor_copy(
                wout_sb[:, h * 4096:(h + 1) * 4096], wostg[:]),
                waits=((wstsm, wstsm.n), (s_te, s_te.n)), inc=s_ve)
        wout_v = s_ve.n
        y2ev = [0, 0]
        bank_fr = [0, 0, 0, 0]
        wadd = [0]
        for ts in range(NQ):
            par = ts % 2
            for jj in range(4):
                j = ts * 4 + jj
                P.on("tensor", lambda e, j=j, jj=jj: [e.transpose(
                    pb[:, ct * 128:(ct + 1) * 128],
                    _ap(v_t, (ct * 128) * 33 + j + 1, [[1024 * 33, 128], [33, 128]]),
                    ident_sb[:]) for ct in range(8)][-1],
                    waits=((s_ve, s_ve.n), (s_ve, y2ev[par]) if y2ev[par] else (s_ve, 0)),
                    inc=None)
                tfence()
                t8 = s_te.n
                P.on("vector", lambda e, par=par, jj=jj: e.tensor_copy(
                    _ap(y2c, par * 4096 + jj * 128, [[2 * 4096, 128], [512, 8], [1, 128]]),
                    pb[:]), waits=((s_te, t8),), inc=s_ve)
            y2ev[par] = s_ve.n
            y2rdy = s_ve.n
            for ncb in range(8):
                bank = ncb % 4
                for dk in range(ND):
                    w = [(s_ve, y2rdy), (s_ve, wout_v)]
                    if dk == 0 and bank_fr[bank] > 0:
                        w.append((s_ac, bank_fr[bank]))
                    if dk == 0:
                        w.append((s_dma, bo_ld))
                        w.append((s_ve, o5_v))
                    P.on("tensor", lambda e, ncb=ncb, dk=dk, bank=bank, par=par:
                         e.matmul(
                             s4[:, bank * 512:(bank + 1) * 512],
                             wout_sb[:, dk * 1024 + ncb * 128:
                                     dk * 1024 + ncb * 128 + 128],
                             y2c[:, par * 4096 + dk * 512: par * 4096 + (dk + 1) * 512],
                             start=(dk == 0), stop=False),
                         waits=tuple(w) if dk == 0 else (), inc=None)
                P.on("tensor", lambda e, ncb=ncb, bank=bank: e.matmul(
                    s4[:, bank * 512:(bank + 1) * 512],
                    bout_row[0:1, ncb * 128:(ncb + 1) * 128],
                    ones512b[0:1, :], start=False, stop=True), inc=s_te)
                pfence()
                P.on("scalar", lambda e, ncb=ncb, bank=bank, par=par: e.activation(
                    woc[:, par * 4096 + ncb * 512: par * 4096 + (ncb + 1) * 512],
                    s4[:, bank * 512:(bank + 1) * 512], AF.Identity),
                    waits=(s_te,), inc=s_ac)
                bank_fr[bank] = s_ac.n
            evd = s_ac.n
            for tsub in range(4):
                sp = tsub % 2
                ti = ts * 4 + tsub
                P.on("sync", lambda e, ti=ti, sp=sp: ld(
                    e, z3rd[:, sp * 1024:(sp + 1) * 1024],
                    z3.ap()[ti * 128:(ti + 1) * 128, :]),
                    waits=((s_ve, wadd[0]), (s_z3h[0], s_z3h[0].n),
                           (s_z3h[1], s_z3h[1].n)), inc=z3sm[sp], inc_k=16)
                zld = (z3sm[sp], z3sm[sp].n)
                P.on("tensor", lambda e, tsub=tsub, par=par: [e.transpose(
                    _ap(o2, k * 128, [[1024, 128], [1, 128]]),
                    woc[:, par * 4096 + k * 512 + tsub * 128:
                        par * 4096 + k * 512 + tsub * 128 + 128],
                    identf_sb[:]) for k in range(8)][-1],
                    waits=((s_ac, evd), (s_ve, wadd[0])), inc=None)
                tfence()
                trd = s_te.n
                P.on("vector", lambda e, sp=sp: e.tensor_add(
                    x4st[:, sp * 1024:(sp + 1) * 1024], o2[:],
                    z3rd[:, sp * 1024:(sp + 1) * 1024]),
                    waits=((s_te, trd), zld, (s_r3h[sp], s_r3h[sp].n)), inc=s_ve)
                wadd[0] = s_ve.n
                P.on("gpsimd", lambda e, ti=ti, sp=sp: ld(
                    e, dst.ap()[ti * 128:(ti + 1) * 128, :],
                    x4st[:, sp * 1024:(sp + 1) * 1024]),
                    waits=((s_ve, wadd[0]),), inc=s_r3h[sp], inc_k=16)
        if final:
            P.on("sync", lambda e: None,
                 waits=((s_r3h[0], s_r3h[0].n), (s_r3h[1], s_r3h[1].n)))

    xn4c = dt_("xn4c", [ND, 128, T], F32)
    if (DEBUG in ("sc", "att1") or DEBUG is True) and "attnb" in PHASES:
        P.on("gpsimd", lambda e: ld(e, dbg["r2"].ap(), r2.ap()),
             waits=((s_dsth[0], s_dsth[0].n), (s_dsth[1], s_dsth[1].n)),
             inc=s_scd, inc_k=16)
    if "attnb" in PHASES:
        emit_attnb(r2, r3 if "ffn2" in PHASES else out_t,
                   final=("ffn2" not in PHASES),
                   src_w=((s_dsth[0], s_dsth[0].n), (s_dsth[1], s_dsth[1].n)))
    if (DEBUG in ("sc", "att1") or DEBUG is True) and "ffn2" in PHASES:
        P.on("gpsimd", lambda e: ld(e, dbg["r3"].ap(), r3.ap()),
             waits=((s_r3h[0], s_r3h[0].n), (s_r3h[1], s_r3h[1].n)),
             inc=s_scd, inc_k=16)
    if "ffn2" in PHASES:
        if "attnb" in PHASES:
            emit_wfft()
        emit_ffn(r3, out_t, xn4c, final=True,
                 src_w=((s_r3h[0], s_r3h[0].n), (s_r3h[1], s_r3h[1].n)))

    if PHASES == ("attn",):
        for ti in range(NT):
            par = ti % 2
            P.on("sync", lambda e, ti=ti, par=par: ld(
                e, xtile[:, par * 1024:(par + 1) * 1024],
                r1.ap()[ti * 128:(ti + 1) * 128, :]),
                waits=((s_r1h[0], s_r1h[0].n), (s_r1h[1], s_r1h[1].n),
                       (s_qwh[par], s_qwh[par].n)),
                inc=lnx[par], inc_k=16)
            P.on("gpsimd", lambda e, ti=ti, par=par: ld(
                e, out_t.ap()[ti * 128:(ti + 1) * 128, :],
                xtile[:, par * 1024:(par + 1) * 1024]),
                waits=(lnx[par],), inc=s_qwh[par], inc_k=16)
        P.on("sync", lambda e: None,
             waits=((s_qwh[0], s_qwh[0].n), (s_qwh[1], s_qwh[1].n)))

    P.emit()
    nc._stack.close()
    return nc


def emit_pv(P, kv, p_, s4, o2, vext, p_sb, s_exp, s_pv, lag=0):
    pj = kv % 3
    need = s_exp.n - lag
    for hh in range(2):
        vrng = _ap(vext, kv * 1040 + (p_ * 2 + hh) * 65,
                   [[NKV * 1040, 128], [1, 65]])
        prng = p_sb[:, pj * 1024 + hh * 512: pj * 1024 + hh * 512 + 512]
        P.on("tensor", lambda e, vrng=vrng, prng=prng, hh=hh, kv=kv: e.matmul(
            _ap(o2, hh * 512, [[1024, 65], [1, 512]]), vrng, prng,
            start=(kv == 0), stop=(kv == NKV - 1)),
            waits=((s_exp, need),) if hh == 0 else (),
            inc=(s_pv if hh == 1 else None))


def kernel(**inputs):
    costab, sintab, pmat, ident, identf = host_consts()
    nc = build()
    base = {
        "w_dk": np.ascontiguousarray(inputs["L0_attn_a_w_dk"], np.float32),
        "b_dk": np.ascontiguousarray(inputs["L0_attn_a_b_dk"], np.float32),
        "w_dv": np.ascontiguousarray(inputs["L0_attn_a_w_dv"], np.float32),
        "b_dv": np.ascontiguousarray(inputs["L0_attn_a_b_dv"], np.float32),
        "w_uk": np.ascontiguousarray(inputs["L0_attn_a_w_uk"], np.float32),
        "b_uk": np.ascontiguousarray(inputs["L0_attn_a_b_uk"], np.float32),
        "w_uv": np.ascontiguousarray(inputs["L0_attn_a_w_uv"], np.float32),
        "b_uv": np.ascontiguousarray(inputs["L0_attn_a_b_uv"], np.float32),
        "w_o": np.ascontiguousarray(inputs["L0_attn_a_w_o"], np.float32),
        "b_o": np.ascontiguousarray(inputs["L0_attn_a_b_o"], np.float32),
        "W_in": np.ascontiguousarray(inputs["L0_attn_b_W_in"], np.float32),
        "b_in": np.ascontiguousarray(inputs["L0_attn_b_b_in"], np.float32),
        "k0": np.ascontiguousarray(np.asarray(inputs["L0_attn_b_k0"]).reshape(D, 128), np.float32),
        "k1": np.ascontiguousarray(np.asarray(inputs["L0_attn_b_k1"]).reshape(D, 128), np.float32),
        "gamma": np.ascontiguousarray(inputs["L0_attn_b_gamma"], np.float32),
        "beta": np.ascontiguousarray(inputs["L0_attn_b_beta"], np.float32),
        "W_out": np.ascontiguousarray(inputs["L0_attn_b_W_out"], np.float32),
        "b_out": np.ascontiguousarray(inputs["L0_attn_b_b_out"], np.float32),
        "ffn_w8": np.ascontiguousarray(inputs["L0_ffn_weight_int8"]),
        "ffn_ws": np.ascontiguousarray(np.asarray(inputs["L0_ffn_weight_scale"]).reshape(D), np.float32),
        "ffn_b": np.ascontiguousarray(inputs["L0_ffn_bias"], np.float32),
        "costab": costab, "sintab": sintab, "pmat": pmat,
        "ident": ident, "identf": identf,
    }
    x = np.asarray(inputs["L0_attn_a_x"])
    in_maps = [dict(base, x=np.ascontiguousarray(x[b], np.float32)) for b in range(8)]
    res = run_bass_kernel_spmd(nc, in_maps, core_ids=list(range(8)))
    if DEBUG:
        kernel.dbg_results = res.results
    out = np.stack([np.asarray(res.results[b]["out"]) for b in range(8)])
    return out.astype(np.float32)
